# version 32
# baseline (speedup 1.0000x reference)
import contextlib
import numpy as np
import ml_dtypes
import concourse.bass as bass
import concourse.mybir as mybir
from concourse.bass_utils import run_bass_kernel_spmd

F32 = mybir.dt.float32
BF16 = mybir.dt.bfloat16
AF = mybir.ActivationFunctionType
ALU = mybir.AluOpType
AX = mybir.AxisListType
BF16NP = ml_dtypes.bfloat16

NCORES = 8
B, L, D = 2, 8192, 1024
TT = 2048
NT = 16
KC = 8
NE = 8
CAP = 2048.0
NG = 4
NSG = 2
NH = 16
NEGINF = -3.0e38

PHASES = dict(ffn1=True, attn=True, ffn2=True)
FFN_SUB = dict(route=True, B=True, level=99, nexp=8, bias=True)
ATT_SUB = dict(heads=tuple(range(16)), rope=True, core=True)
DEBUG = False
DEBUG2 = False



def qk_col_perm():
    q = np.zeros(1024, dtype=np.int64)
    for c in range(4):
        for half in range(2):
            for hh in range(4):
                h = 4 * c + hh
                for j in range(32):
                    q[(2 * c + half) * 128 + 32 * hh + j] = 64 * h + 2 * j + half
    perm = np.arange(3072, dtype=np.int64)
    perm[0:1024] = q
    perm[1024:2048] = 1024 + q
    return perm


def rope_tables(core):
    j = (np.arange(128) % 32).astype(np.float32)
    inv = (np.float32(1.0) / np.float32(10000.0) ** ((2.0 * j).astype(np.float32) / np.float32(64.0))).astype(np.float32)
    pos = ((core % 4) * TT + np.arange(TT)).astype(np.float32)
    t = pos[None, :].astype(np.float32) * inv[:, None]
    return np.cos(t).astype(np.float32), np.sin(t).astype(np.float32)


def build_consts():
    ident = np.eye(128, dtype=np.float32)
    ident_bf = np.eye(128, dtype=BF16NP)
    r = np.arange(128)
    tri = np.where(r[None, :] <= r[:, None], 0.0, NEGINF).astype(np.float32)
    m = np.zeros((128, 384), dtype=np.float32)
    m[:, 0:128] = tri
    m[:, 256:384] = tri
    mask384 = m.astype(BF16NP)
    sel = np.zeros((8, 1024), dtype=np.float32)
    for e in range(8):
        sel[e, e * 128:(e + 1) * 128] = 1.0
    return ident, ident_bf, mask384, sel.astype(BF16NP)



class Prog:
    CE = ("pe", "act", "dve")
    QE = ("qs", "qg", "qa")

    def __init__(self):
        self.ops = []
        self.last_w = {}
        self.readers = {}
        self.pending_barrier = {}

    def op(self, eng, fn, reads=(), writes=(), sem=None):
        idx = len(self.ops)
        deps = set()
        for rsc in reads:
            w = self.last_w.get(rsc)
            if w is not None:
                deps.add(w)
            if rsc.startswith("pb"):
                for rd in self.readers.get(rsc, ()):
                    if self.ops[rd]["eng"] != eng:
                        deps.add(rd)
        for rsc in writes:
            w = self.last_w.get(rsc)
            if w is not None:
                deps.add(w)
            deps.update(self.readers.get(rsc, ()))
        if eng in self.pending_barrier:
            deps |= self.pending_barrier.pop(eng)
        self.ops.append(dict(idx=idx, eng=eng, fn=fn, deps=deps, sem=sem, marked=False))
        for rsc in reads:
            self.readers.setdefault(rsc, []).append(idx)
        for rsc in writes:
            self.last_w[rsc] = idx
            self.readers[rsc] = []
        return idx

    def barrier(self):
        last = {}
        for o in self.ops:
            last[o["eng"]] = o["idx"]
        deps = set(last.values())
        for eng in list(self.CE) + list(self.QE):
            self.pending_barrier.setdefault(eng, set()).update(deps)

    def finalize(self):
        for o in self.ops:
            if o["eng"] in self.QE:
                o["marked"] = True
            best = {}
            qdeps = set()
            for d in o["deps"]:
                dep = self.ops[d]
                if dep["eng"] == o["eng"] and dep["sem"] is None and o["eng"] == "pe":
                    continue
                if dep["sem"] is not None or dep["eng"] in self.QE:
                    qdeps.add(d)
                else:
                    k = dep["eng"]
                    if k not in best or best[k] < d:
                        best[k] = d
            fd = qdeps | set(best.values())
            o["fdeps"] = fd
            for d in fd:
                self.ops[d]["marked"] = True
        cnt = {}
        for o in self.ops:
            if o["eng"] in self.QE:
                if o["sem"] is None:
                    o["qidx"] = cnt.get(o["eng"], 0)
                    cnt[o["eng"]] = o["qidx"] + 1
                else:
                    o["qidx"] = None
                o["mcount"] = cnt.get(o["eng"], 0)
            else:
                if o["marked"]:
                    cnt[o["eng"]] = cnt.get(o["eng"], 0) + 1
                o["mcount"] = cnt.get(o["eng"], 0)


def build_nc():
    nc = bass.Bass("TRN2", target_bir_lowering=False)

    dx = nc.dram_tensor("x", [TT, D], F32, kind="ExternalInput").ap()
    dwqkv = nc.dram_tensor("wqkvT", [D, 3072], F32, kind="ExternalInput").ap()
    dwout = nc.dram_tensor("woutT", [D, D], F32, kind="ExternalInput").ap()
    dwexp = nc.dram_tensor("wexpT", [NE, D, D], F32, kind="ExternalInput").ap()
    dwg = nc.dram_tensor("wg", [D, NE], F32, kind="ExternalInput").ap()
    dbias = nc.dram_tensor("bias", [NE, D], F32, kind="ExternalInput").ap()
    dcos = nc.dram_tensor("cos", [128, TT], F32, kind="ExternalInput").ap()
    dsin = nc.dram_tensor("sin", [128, TT], F32, kind="ExternalInput").ap()
    dident = nc.dram_tensor("ident", [128, 128], F32, kind="ExternalInput").ap()
    didbf = nc.dram_tensor("ident_bf", [128, 128], BF16, kind="ExternalInput").ap()
    dmask = nc.dram_tensor("mask384", [128, 384], BF16, kind="ExternalInput").ap()
    dsel = nc.dram_tensor("sel", [8, 1024], BF16, kind="ExternalInput").ap()
    deidx = nc.dram_tensor("eidx", [128, 8], F32, kind="ExternalInput").ap()
    dtokbig = nc.dram_tensor("tokbig", [128, NT], F32, kind="ExternalInput").ap()
    dones1 = nc.dram_tensor("ones1", [1, 128], F32, kind="ExternalInput").ap()
    drank = nc.dram_tensor("rankmask", [8, 1], F32, kind="ExternalInput").ap()
    dy = nc.dram_tensor("y", [TT, D], F32, kind="ExternalOutput").ap()

    KCAP = 512
    NSLOT = NE * KCAP
    z_scr = nc.dram_tensor("z_scr", [NSLOT, D], BF16, kind="ExternalInput")
    y_scr = nc.dram_tensor("y_scr", [NSLOT, D], F32)
    if DEBUG:
        dbg_logit = nc.dram_tensor("dbg_logit", [128, NT * 8], F32, kind="ExternalOutput").ap()
        dbg_msk = nc.dram_tensor("dbg_msk", [128, NT * 8], F32, kind="ExternalOutput").ap()
        dbg_scan = nc.dram_tensor("dbg_scan", [8, TT], F32, kind="ExternalOutput").ap()
        dbg_cT = nc.dram_tensor("dbg_cT", [8, TT], BF16, kind="ExternalOutput").ap()
        dbg_gate = nc.dram_tensor("dbg_gate", [128, NT], F32, kind="ExternalOutput").ap()
        dbg_cb = nc.dram_tensor("dbg_cb", [128, NE * 512], BF16, kind="ExternalOutput").ap()
        dbg_xln = nc.dram_tensor("dbg_xln", [128, KC * TT], BF16, kind="ExternalOutput").ap()
        dbg_coff = nc.dram_tensor("dbg_coff", [8, 1], F32, kind="ExternalOutput").ap()
        dbg_wb = nc.dram_tensor("dbg_wb", [128, 8192], BF16, kind="ExternalOutput").ap()
        dbg_z = nc.dram_tensor("dbg_z", [128, KC * 512], BF16, kind="ExternalOutput").ap()
        dbg_ps = nc.dram_tensor("dbg_ps", [128, 1024], F32, kind="ExternalOutput").ap()
    if DEBUG2:
        dbg_sidx = nc.dram_tensor("dbg_sidx", [128, NT], F32, kind="ExternalOutput").ap()
        dbg_gidx = nc.dram_tensor("dbg_gidx", [128, NT], F32, kind="ExternalOutput").ap()
        dbg_xlnc = nc.dram_tensor("dbg_xlnc", [128, NT * D], BF16, kind="ExternalOutput").ap()
    cc_in = [nc.dram_tensor(f"cc_in{i}", [1, 8], F32) for i in range(2)]
    cc_out = [nc.dram_tensor(f"cc_out{i}", [8, 8], F32) for i in range(2)]

    P = Prog()
    REGS = {}
    es = contextlib.ExitStack()
    with es:
        def sb(name, shape, dt):
            return es.enter_context(nc.sbuf_tensor(name, shape, dt))

        x_sb = sb("x_sb", [128, NT * D], F32)
        s_id = sb("s_id", [128, 128], F32)
        s_idbf = sb("s_idbf", [128, 128], BF16)
        s_mask = sb("s_mask", [128, 384], BF16)
        s_wg = sb("s_wg", [128, 64], F32)
        s_bias = sb("s_bias", [8, D], BF16)
        s_rank = sb("s_rank", [8, 1], F32)
        s_eps = sb("s_eps", [128, 1], F32)
        scrA = [sb(f"scrA{i}", [128, D], F32) for i in range(2)]
        scr_b = sb("scr_b", [128, D], F32)
        c_ssum = sb("c_ssum", [128, NT], F32)
        c_sumsq = sb("c_sumsq", [128, NT], F32)
        c_nmean = sb("c_nmean", [128, NT], F32)
        c_rstd = sb("c_rstd", [128, NT], F32)
        c_var = sb("c_var", [128, 4], F32)
        c_gate = sb("c_gate", [128, NT], F32)
        c_tmp = sb("c_tmp", [128, 4], F32)
        c_rt = sb("c_rt", [128, 4], F32)
        c_posel = sb("c_posel", [128, 2], F32)
        c_cc = sb("c_cc", [128, 4], F32)
        s_pos8 = sb("s_pos8", [128, 16], F32)
        s_cnt = sb("s_cnt", [8, 1], F32)
        s_gath = sb("s_gath", [8, 8], F32)
        s_coff = sb("s_coff", [8, 1], F32)
        s_eidx = sb("s_eidx", [128, 8], F32)
        s_ones1 = sb("s_ones1", [1, 128], F32)
        s_c4096 = sb("s_c4096", [128, 1], F32)
        s_coffb = sb("s_coffb", [128, 8], F32)
        s_crow = sb("s_crow", [1, 8], F32)
        c_e2 = sb("c_e2", [128, 2], F32)
        c_poselN = sb("c_poselN", [128, NT], F32)
        c_all = sb("c_all", [128, NT], F32)
        c_lp = sb("c_lp", [128, 2], F32)
        c_ov = sb("c_ov", [128, 2], F32)
        c_sl = sb("c_sl", [128, 2], F32)

        p4 = [es.enter_context(nc.psum_tensor(f"p4_{i}", [128, 1024], F32)) for i in range(4)]
        PB = lambda i, off: f"pb{2 * i + (1 if off >= 512 else 0)}"

        sem = {n: es.enter_context(nc.semaphore(f"sem_{n}")) for n in ("pe", "act", "dve")}
        QDEPTHS = {"qs": 6, "qg": 8, "qa": 2}
        qsems = {q: [es.enter_context(nc.semaphore(f"sem_{q}{i}")) for i in range(QDEPTHS[q])]
                 for q in ("qs", "qg", "qa")}
        cc_sems = [es.enter_context(nc.semaphore(f"cc{i}")) for i in range(2)]
        block = es.enter_context(nc.Block())

        for t in range(NT):
            P.op("qs", lambda e, t=t: e.dma_start(
                out=x_sb[:, t * D:(t + 1) * D], in_=dx[t * 128:(t + 1) * 128, :]),
                writes=[f"xt{t}"])
        for dst, src, key in ((s_id, dident, "id"), (s_idbf, didbf, "idbf"),
                              (s_mask, dmask, "mask"),
                              (s_rank, drank, "rank")):
            P.op("qs", lambda e, d=dst, s=src: e.dma_start(out=d[:, :], in_=s[:, :]), writes=[key])
        P.op("qs", lambda e: e.dma_start(out=s_wg.ap().rearrange("p (k e) -> p k e", k=8),
                                         in_=dwg.rearrange("(k p) e -> p k e", p=128)), writes=["wg"])
        P.op("qg", lambda e: e.dma_start(out=s_bias[:, :], in_=dbias[:, :]),
            writes=[f"biasrow{e_}" for e_ in range(NE)])
        P.op("dve", lambda e: e.memset(s_eps[:, :], 1e-5), writes=["eps"])
        P.op("dve", lambda e: e.memset(s_c4096[:, :], float(NSLOT)), writes=["c4096"])
        P.op("qs", lambda e: e.dma_start(out=s_eidx[:, :], in_=deidx[:, :]), writes=["eidx"])
        P.op("qs", lambda e: e.dma_start(out=s_ones1[:, :], in_=dones1[:, :]), writes=["ones1"])

        def sw_pipe(n_items, stages):
            for i in range(n_items + len(stages) - 1):
                for j in range(len(stages) - 1, -1, -1):
                    idx = i - j
                    if 0 <= idx < n_items:
                        stages[j](idx)

        def ln_stages(t):
            xk = f"xt{t}"
            q4 = t % 4
            par = t % 2
            def st0(_=None):
                P.op("act", lambda e, t=t: e.activation(
                    out=scr_b[:, :], in_=x_sb[:, t * D:(t + 1) * D], func=AF.Square,
                    accum_out=c_sumsq[:, t:t + 1]),
                    reads=[xk], writes=["scrb", f"sumsq{t}"])
                P.op("act", lambda e, t=t: e.activation(
                    out=scr_b[:, :], in_=x_sb[:, t * D:(t + 1) * D], func=AF.Copy,
                    bias=0.0, accum_out=c_ssum[:, t:t + 1]),
                    reads=[xk], writes=["scrb", f"ssum{t}"])
            def st1(_=None):
                P.op("dve", lambda e, t=t: e.tensor_scalar(
                    out=c_nmean[:, t:t + 1], in0=c_ssum[:, t:t + 1],
                    scalar1=-1.0 / D, scalar2=None, op0=ALU.mult),
                    reads=[f"ssum{t}"], writes=[f"nmean{t}"])
                P.op("dve", lambda e, t=t, q4=q4: e.tensor_tensor(
                    out=c_var[:, q4:q4 + 1], in0=c_nmean[:, t:t + 1],
                    in1=c_nmean[:, t:t + 1], op=ALU.mult),
                    reads=[f"nmean{t}"], writes=[f"var{q4}"])
                P.op("dve", lambda e, t=t, q4=q4: e.scalar_tensor_tensor(
                    out=c_var[:, q4:q4 + 1], in0=c_sumsq[:, t:t + 1], scalar=1.0 / D,
                    in1=c_var[:, q4:q4 + 1], op0=ALU.mult, op1=ALU.subtract),
                    reads=[f"sumsq{t}", f"var{q4}"], writes=[f"var{q4}"])
            def st2(_=None):
                P.op("act", lambda e, q4=q4: e.activation(
                    out=c_tmp[:, q4:q4 + 1], in_=c_var[:, q4:q4 + 1], func=AF.Ln,
                    bias=s_eps[:, :]),
                    reads=[f"var{q4}", "eps"], writes=[f"ctmp{q4}"])
                P.op("act", lambda e, t=t, q4=q4: e.activation(
                    out=c_rstd[:, t:t + 1], in_=c_tmp[:, q4:q4 + 1], func=AF.Exp, scale=-0.5),
                    reads=[f"ctmp{q4}"], writes=[f"rstd{t}"])
            def st3(_=None):
                P.op("dve", lambda e, t=t, par=par: e.tensor_scalar(
                    out=scrA[par][:, :], in0=x_sb[:, t * D:(t + 1) * D],
                    scalar1=c_nmean[:, t:t + 1], scalar2=c_rstd[:, t:t + 1],
                    op0=ALU.add, op1=ALU.mult),
                    reads=[xk, f"nmean{t}", f"rstd{t}"], writes=[f"scrA{par}"])
            return [st0, st1, st2, st3]

        def mk_l2_stages(t, ts, xln2a, v_a, wbig_t):
            par = t % 2
            sts = list(ln_stages(t))

            def s_tp(_=None):
                pa = p4[par]
                pak = [PB(par, 0), PB(par, 512)]
                for k in range(KC):
                    P.op("pe", lambda e, k=k, pa=pa, par=par: e.transpose(
                        pa[:, k * 128:(k + 1) * 128], scrA[par][:, k * 128:(k + 1) * 128],
                        s_id[:, :]),
                        reads=[f"scrA{par}", "id"], writes=pak)

            def s_cp(_=None):
                pa = p4[par]
                pak = [PB(par, 0), PB(par, 512)]
                for k in range(KC):
                    P.op("dve", lambda e, ts=ts, k=k, pa=pa: e.tensor_copy(
                        out=xln2a[:, k * 1024 + ts * 128: k * 1024 + (ts + 1) * 128],
                        in_=pa[:, k * 128:(k + 1) * 128]),
                        reads=pak, writes=[f"xl2_{ts}"])

            def s_v(_=None):
                pv = p4[2 + ts % 2]
                pvk = [PB(2 + ts % 2, 0), PB(2 + ts % 2, 512)]
                for k in range(KC):
                    for os_ in range(2):
                        P.op("pe", lambda e, ts=ts, k=k, os_=os_, pv=pv: e.matmul(
                            pv[:, os_ * 512:(os_ + 1) * 512],
                            xln2a[:, k * 1024 + ts * 128: k * 1024 + (ts + 1) * 128],
                            wbig_t[:, k * 1024 + os_ * 512: k * 1024 + (os_ + 1) * 512],
                            start=(k == 0), stop=(k == KC - 1)),
                            reads=[f"xl2_{ts}", "wbig"],
                            writes=[pvk[os_]])

            def s_ve(_=None):
                pv = p4[2 + ts % 2]
                pvk = [PB(2 + ts % 2, 0), PB(2 + ts % 2, 512)]
                P.op("act", lambda e, ts=ts, pv=pv: e.copy(
                    out=v_a[:, ts * 1024:(ts + 1) * 1024], in_=pv[:, :]),
                    reads=pvk, writes=[f"v{ts}"])
            return sts + [s_tp, s_cp, s_v, s_ve]

        def ffn(phase):
            fs = contextlib.ExitStack()
            foff = {}
            with fs:
                def fsb(name, shape, dt):
                    foff[name] = nc.sbuf_base
                    return fs.enter_context(nc.sbuf_tensor(f"{name}_f{phase}", shape, dt))
                xlnt = fsb("xlnt", [128, NT * D], BF16)
                wbuf = [fsb(f"wb{i}", [128, 8192], BF16) for i in range(2)]
                gbuf = [fsb(f"gb{i}", [128, D], BF16) for i in range(2)]
                zfm = [fsb(f"zf{i}", [128, D], BF16) for i in range(2)]
                sidx = fsb("sidx", [128, NT], mybir.dt.uint32)
                s_logit = fsb("logit", [128, NT * 8], F32)
                s_msk = fsb("msk", [128, NT * 8], F32)
                ybuf = [fsb(f"yb{i}", [128, D], F32) for i in range(2)]
                cgb = [fsb(f"cgb{i}", [128, D], F32) for i in range(2)]
                maskT = fsb("maskT", [8, TT], F32)
                scanT = fsb("scanT", [8, TT], F32)
                xfm32 = [fsb(f"xfm32_{i}", [128, D], F32) for i in range(2)]
                ones8 = fsb("ones8", [8, 128], F32)
                s_sel = fsb("s_sel", [8, 1024], BF16)
                ci = phase - 1
                ph = f"p{phase}_"

                P.op("dve", lambda e: e.memset(ones8[:, :], 1.0), writes=["ones8g"])
                P.op("qs", lambda e: e.dma_start(out=s_sel[:, :], in_=dsel[:, :]),
                    writes=["selc"])

                def load_expert(e_):
                    P.op("qg", lambda e, ei=e_: e.dma_start(
                        out=wbuf[(ei + 1) % 2][:, :].rearrange("p (k f) -> p k f", k=8),
                        in_=dwexp[ei].rearrange("(k p) f -> p k f", p=128)),
                        writes=[ph + f"wbuf{(e_ + 1) % 2}"])

                def a_stages(t):
                    par = t % 2
                    q4 = t % 4
                    lgo = par * 512
                    lg = s_logit[:, t * 8:(t + 1) * 8]
                    sts = list(ln_stages(t))
                    def s_tp(_=None):
                        pa = p4[par]
                        pak = [PB(par, 0), PB(par, 512)]
                        for k in range(KC):
                            P.op("pe", lambda e, k=k, pa=pa, par=par: e.transpose(
                                pa[:, k * 128:(k + 1) * 128], scrA[par][:, k * 128:(k + 1) * 128],
                                s_id[:, :]),
                                reads=[f"scrA{par}", "id"], writes=pak)
                    def s_ev(_=None):
                        pa = p4[par]
                        pak = [PB(par, 0), PB(par, 512)]
                        P.op("act", lambda e, pa=pa, par=par: e.copy(out=xfm32[par][:, :], in_=pa[:, :]),
                            reads=pak, writes=[ph + f"xfm32{par}"])
                    def s_cl(_=None):
                        for k in range(KC):
                            P.op("pe", lambda e, k=k, lgo=lgo, par=par: e.matmul(
                                p4[2][:, lgo:lgo + 8], xfm32[par][:, k * 128:(k + 1) * 128],
                                s_wg[:, k * 8:(k + 1) * 8], start=(k == 0), stop=(k == KC - 1)),
                                reads=[ph + f"xfm32{par}", "wg"],
                                writes=[PB(2, lgo), ph + f"lg{par}"])
                    def s_le(_=None):
                        P.op("act", lambda e, t=t, lgo=lgo: e.copy(
                            out=s_logit[:, t * 8:(t + 1) * 8], in_=p4[2][:, lgo:lgo + 8]),
                            reads=[PB(2, lgo), ph + f"lg{par}"], writes=[ph + f"lgt{t}"])
                    def s_rt(_=None):
                        P.op("dve", lambda e, lg=lg, t=t, q4=q4: e.reduce_max(
                            out=c_rt[:, q4:q4 + 1], in_=lg, axis=AX.X),
                            reads=[ph + f"lgt{t}"], writes=[f"crt{q4}"])
                        P.op("dve", lambda e, lg=lg, t=t, q4=q4: e.tensor_scalar(
                            out=s_msk[:, t * 8:(t + 1) * 8], in0=lg,
                            scalar1=c_rt[:, q4:q4 + 1], scalar2=None, op0=ALU.is_equal),
                            reads=[ph + f"lgt{t}", f"crt{q4}"], writes=[ph + f"msk{t}"])
                        P.op("dve", lambda e, q4=q4: e.tensor_scalar(
                            out=c_cc[:, q4:q4 + 1], in0=c_rt[:, q4:q4 + 1],
                            scalar1=-1.0, scalar2=float(np.log(2.0)), op0=ALU.mult,
                            op1=ALU.add),
                            reads=[f"crt{q4}"], writes=[f"ccc{q4}"])
                    def s_ex(_=None):
                        P.op("act", lambda e, lg=lg, t=t, q4=q4, par=par: e.activation(
                            out=s_pos8[:, par * 8:par * 8 + 8], in_=lg, func=AF.Exp,
                            bias=c_cc[:, q4:q4 + 1], accum_out=c_rt[:, q4:q4 + 1]),
                            reads=[ph + f"lgt{t}", f"ccc{q4}"], writes=[f"pos8{par}", f"crt{q4}"])
                    def s_gt(_=None):
                        P.op("dve", lambda e, t=t, q4=q4: e.reciprocal(
                            out=c_gate[:, t:t + 1], in_=c_rt[:, q4:q4 + 1]),
                            reads=[f"crt{q4}"], writes=[f"gate{t}"])
                    def s_mt(_=None):
                        mto = par * 512
                        P.op("pe", lambda e, t=t, mto=mto: e.transpose(
                            p4[3][0:8, mto:mto + 128], s_msk[:, t * 8:(t + 1) * 8], s_id[:, :]),
                            reads=[ph + f"msk{t}", "id"], writes=[PB(3, mto)])
                        P.op("act", lambda e, t=t, mto=mto: e.copy(
                            out=maskT[:, t * 128:(t + 1) * 128], in_=p4[3][0:8, mto:mto + 128]),
                            reads=[PB(3, mto)], writes=[ph + f"maskT{t}"])
                    orig3 = sts[3]
                    def st3b(_=None):
                        orig3()
                        P.op("dve", lambda e, t=t, par=par: e.tensor_copy(
                            out=xlnt[:, t * D:(t + 1) * D], in_=scrA[par][:, :]),
                            reads=[f"scrA{par}"], writes=[ph + f"xlnt{t}"])
                    sts[3] = st3b
                    lvl = FFN_SUB["level"]
                    if lvl < -1.5:
                        return sts
                    sts += [s_tp]
                    if lvl >= -0.5:
                        sts += [s_ev]
                    if lvl >= 0:
                        sts += [s_cl]
                    if lvl >= 2:
                        sts += [s_le, s_rt, s_ex, s_gt, s_mt]
                    return sts

                def z0(t):
                    par = t % 2
                    if par == 0:
                        P.op("pe", lambda e, t=t: e.transpose(
                            p4[2][:, 544:552], scanT[:, t * 128:(t + 1) * 128], s_id[0:8, 0:8]),
                            reads=[ph + f"scanT{t}", "id"], writes=[PB(2, 512), ph + "posE"])
                    else:
                        P.op("pe", lambda e, t=t: e.transpose(
                            p4[3][:, 256:264], scanT[:, t * 128:(t + 1) * 128], s_id[0:8, 0:8]),
                            reads=[ph + f"scanT{t}", "id"], writes=[PB(3, 0), ph + "posO"])

                def z1(t):
                    par = t % 2
                    pos_ap = p4[2][:, 544:552] if par == 0 else p4[3][:, 256:264]
                    posk = [ph + "posE", PB(2, 512)] if par == 0 else [ph + "posO", PB(3, 0)]
                    P.op("dve", lambda e, t=t, par=par, pos_ap=pos_ap: e.scalar_tensor_tensor(
                        out=s_pos8[:, par * 8:par * 8 + 8], in0=pos_ap, scalar=1.0,
                        in1=s_msk[:, t * 8:(t + 1) * 8], op0=ALU.mult, op1=ALU.mult,
                        accum_out=c_poselN[:, t:t + 1]),
                        reads=posk + [ph + f"msk{t}"], writes=[f"pos8{par}", f"poseln{t}"])
                    P.op("dve", lambda e, t=t, par=par: e.scalar_tensor_tensor(
                        out=s_pos8[:, par * 8:par * 8 + 8], in0=s_msk[:, t * 8:(t + 1) * 8],
                        scalar=1.0, in1=s_eidx[:, :], op0=ALU.mult, op1=ALU.mult,
                        accum_out=c_e2[:, par:par + 1]),
                        reads=[ph + f"msk{t}", "eidx"], writes=[f"pos8{par}", f"ce2{par}"])
                    P.op("dve", lambda e, t=t, par=par: e.tensor_scalar(
                        out=c_sl[:, par:par + 1], in0=c_poselN[:, t:t + 1],
                        scalar1=8.0, scalar2=c_e2[:, par:par + 1],
                        op0=ALU.mult, op1=ALU.add),
                        reads=[f"ce2{par}", f"poseln{t}"], writes=[f"csl{par}"])
                    P.op("dve", lambda e, t=t, par=par: e.tensor_copy(
                        out=sidx[:, t:t + 1], in_=c_sl[:, par:par + 1]),
                        reads=[f"csl{par}"], writes=[ph + f"sidx{t}"])

                def z4(t):
                    P.op("qg", lambda e, t=t: e.indirect_dma_start(
                        out=z_scr.ap(),
                        out_offset=bass.IndirectOffsetOnAxis(ap=sidx[:, t:t + 1], axis=0),
                        in_=xlnt[:, t * D:(t + 1) * D], in_offset=None,
                        bounds_check=REGS["sb"], oob_is_err=False),
                        reads=[ph + f"xlnt{t}", ph + f"sidx{t}"], writes=[ph + f"zsc{t}"])
                    if t == 1 or t == 3:
                        load_expert(t // 2)
                def s_scan(t):
                    init = 0.0 if t == 0 else scanT[:, t * 128 - 1: t * 128]
                    rd = ["ones8g", ph + f"maskT{t}"] + ([] if t == 0 else [ph + f"scanT{t-1}"])
                    P.op("dve", lambda e, t=t, init=init: e.tensor_tensor_scan(
                        out=scanT[:, t * 128:(t + 1) * 128], data0=ones8[:, 0:128],
                        data1=maskT[:, t * 128:(t + 1) * 128],
                        initial=init, op0=ALU.mult, op1=ALU.add),
                        reads=rd, writes=[ph + f"scanT{t}"])
                stage_lists = [a_stages(t) for t in range(NT)]
                nst_a = max(len(s) for s in stage_lists)
                stages_a = [lambda idx, j=j: stage_lists[idx][j]() if j < len(stage_lists[idx]) else None
                            for j in range(nst_a)]
                if FFN_SUB["level"] >= 2 and FFN_SUB["route"]:
                    stages_a = stages_a + [s_scan, z0, z1, z4]
                sw_pipe(NT, stages_a)
                if FFN_SUB["level"] < 2:
                    return
                if not FFN_SUB["route"]:
                    return
                P.op("act", lambda e: e.copy(out=s_cnt[:, :], in_=scanT[:, TT - 1:TT]),
                    reads=[ph + f"scanT{NT-1}"], writes=["cnt"])
                P.op("qa", lambda e, ci=ci: e.dma_start(
                    out=cc_in[ci].ap().rearrange("a e -> e a"), in_=s_cnt[:, :]),
                    reads=["cnt"], writes=[f"ccin{ci}"])
                P.op("qg", lambda e, ci=ci: e.collective_compute(
                    "AllGather", ALU.bypass, replica_groups=[list(range(NCORES))],
                    ins=[cc_in[ci].ap().opt()], outs=[cc_out[ci].ap().opt()]),
                    reads=[f"ccin{ci}"], writes=[f"ccout{ci}"], sem=cc_sems[ci])
                P.op("qa", lambda e, ci=ci: e.dma_start(out=s_gath[:, :], in_=cc_out[ci][:, :]),
                    reads=[f"ccout{ci}"], writes=["gath"])

                if DEBUG2 and phase == 1:
                    P.op("qg", lambda e: e.dma_start(out=dbg_sidx[:, :], in_=sidx[:, :]),
                        reads=[ph + f"sidx{t2}" for t2 in range(NT)], writes=["dd1"])
                    P.op("qs", lambda e: e.dma_start(out=dbg_xlnc[:, :], in_=xlnt[:, :]),
                        reads=[ph + f"xlnt{t2}" for t2 in range(NT)], writes=["dd3"])

                if not FFN_SUB["B"]:
                    return
                SPE = KCAP // 128
                NST = NE * SPE
                zsc_all = [ph + f"zsc{t}" for t in range(NT)]

                zr8 = z_scr.ap().rearrange("(p e) f -> e p f", e=8)
                yr8 = y_scr.ap()[0:NSLOT, :].rearrange("(p e) f -> e p f", e=8)

                def b_load(st):
                    sl = st % 2
                    e_, i_ = st // SPE, st % SPE
                    P.op("qs", lambda e, e_=e_, i_=i_, sl=sl: e.dma_start(
                        out=gbuf[sl][:, :], in_=zr8[e_][i_ * 128:(i_ + 1) * 128, :]),
                        reads=zsc_all, writes=[ph + f"gb{sl}"])

                def b_tp(st):
                    sl = st % 2
                    pbf = p4[st % 2][:, 0:512].bitcast(BF16)
                    pak = [PB(st % 2, 0)]
                    for k in range(KC):
                        P.op("pe", lambda e, k=k, sl=sl, pbf=pbf: e.transpose(
                            pbf[:, k * 128:(k + 1) * 128], gbuf[sl][:, k * 128:(k + 1) * 128],
                            s_idbf[:, :]),
                            reads=[ph + f"gb{sl}", "idbf"], writes=pak)
                    P.op("act", lambda e, sl=sl, pbf=pbf: e.copy(out=zfm[sl][:, :], in_=pbf[:, :]),
                        reads=pak, writes=[ph + f"zf{sl}"])

                def b_mm(st):
                    sl = st % 2
                    e_ = st // SPE
                    pm = p4[2 + st % 2]
                    for k in range(KC):
                        for os_ in range(2):
                            P.op("pe", lambda e, k=k, os_=os_, sl=sl, pm=pm: e.matmul(
                                pm[:, os_ * 512:(os_ + 1) * 512],
                                zfm[sl][:, k * 128:(k + 1) * 128],
                                wbuf[(e_ + 1) % 2][:, k * 1024 + os_ * 512: k * 1024 + (os_ + 1) * 512],
                                start=(k == 0), stop=False),
                                reads=[ph + f"zf{sl}", ph + f"wbuf{(e_ + 1) % 2}"],
                                writes=[PB(2 + st % 2, os_ * 512)])
                    for os_ in range(2):
                        P.op("pe", lambda e, os_=os_, e_=e_, pm=pm: e.matmul(
                            pm[:, os_ * 512:(os_ + 1) * 512],
                            s_sel[0:8, e_ * 128:(e_ + 1) * 128],
                            s_bias[0:8, os_ * 512:(os_ + 1) * 512],
                            start=False, stop=True),
                            reads=["selc", f"biasrow{e_}"],
                            writes=[PB(2 + st % 2, os_ * 512)])
                    if st % SPE == SPE - 1 and e_ + 2 < NE:
                        load_expert(e_ + 2)

                def b_ev(st):
                    sl = st % 2
                    pm = p4[2 + st % 2]
                    P.op("act", lambda e, sl=sl, pm=pm: e.copy(out=ybuf[sl][:, :], in_=pm[:, :]),
                        reads=[PB(2 + st % 2, 0), PB(2 + st % 2, 512)], writes=[ph + f"yb{sl}"])

                def b_st(st):
                    sl = st % 2
                    e_, i_ = st // SPE, st % SPE
                    P.op("qs", lambda e, e_=e_, i_=i_, sl=sl, st=st: e.dma_start(
                        out=yr8[e_][i_ * 128:(i_ + 1) * 128, :], in_=ybuf[sl][:, :]),
                        reads=[ph + f"yb{sl}"], writes=[ph + f"yst{st}"])

                def y1(t):
                    par = t % 2
                    P.op("dve", lambda e, t=t, par=par: e.scalar_tensor_tensor(
                        out=s_pos8[:, par * 8:par * 8 + 8], in0=s_msk[:, t * 8:(t + 1) * 8],
                        scalar=1.0, in1=s_coffb[:, :], op0=ALU.mult, op1=ALU.mult,
                        accum_out=c_lp[:, par:par + 1]),
                        reads=[ph + f"msk{t}", "coffb"], writes=[f"pos8{par}", f"clp{par}"])
                    P.op("dve", lambda e, t=t, par=par: e.tensor_tensor(
                        out=c_lp[:, par:par + 1], in0=c_poselN[:, t:t + 1],
                        in1=c_lp[:, par:par + 1], op=ALU.add),
                        reads=[f"poseln{t}", f"clp{par}"], writes=[f"clp{par}"])
                    P.op("dve", lambda e, t=t, par=par: e.scalar_tensor_tensor(
                        out=c_all[:, t:t + 1], in0=c_lp[:, par:par + 1], scalar=CAP,
                        in1=c_gate[:, t:t + 1], op0=ALU.is_lt, op1=ALU.mult),
                        reads=[f"clp{par}", f"gate{t}"], writes=[f"call{t}"])

                def emit_coff_chain():
                    P.op("pe", lambda e: e.matmul(p4[0][0:8, 512:513], s_gath[:, :], s_rank[:, :],
                                                  start=True, stop=True),
                        reads=["gath", "rank"], writes=[PB(0, 512), ph + "coffR"])
                    P.op("act", lambda e: e.copy(out=s_coff[:, :], in_=p4[0][0:8, 512:513]),
                        reads=[PB(0, 512), ph + "coffR"], writes=["coff"])
                    P.op("pe", lambda e: e.transpose(
                        p4[0][0:1, 520:528], s_coff[:, :], s_id[0:8, 0:8]),
                        reads=["coff", "id"], writes=[PB(0, 512), ph + "crowR"])
                    P.op("act", lambda e: e.copy(out=s_crow[:, :], in_=p4[0][0:1, 520:528]),
                        reads=[PB(0, 512), ph + "crowR"], writes=["crow"])
                    P.op("pe", lambda e: e.matmul(p4[0][:, 528:536], s_ones1[:, :], s_crow[:, :],
                                                  start=True, stop=True),
                        reads=["ones1", "crow"], writes=[PB(0, 512), ph + "cofbR"])
                    P.op("act", lambda e: e.copy(out=s_coffb[:, :], in_=p4[0][:, 528:536]),
                        reads=[PB(0, 512), ph + "cofbR"], writes=["coffb"])

                def b_hook(st):
                    if st == 12:
                        emit_coff_chain()
                    elif st == 14:
                        for t_ in range(NT):
                            y1(t_)
                    elif st == 28 and phase == 1 and PHASES["attn"]:
                        P.op("qg", lambda e: e.dma_start(
                            out=wbuf[1][:, :].rearrange("p (k f) -> p k f", k=8),
                            in_=dwqkv.rearrange("(k p) c -> p k c", p=128)[:, :, 2048:3072]),
                            writes=["wbig", ph + "wbuf1"])

                sw_pipe(NST, [b_load, b_tp, b_mm, b_ev, b_st, b_hook])

                yst_all = [ph + f"yst{st}" for st in range(NST)]

                ygb4 = [cgb[0], cgb[1], ybuf[0], ybuf[1]]

                def c_gather(t):
                    sl = t % 4
                    P.op("qg", lambda e, t=t, sl=sl: e.indirect_dma_start(
                        out=ygb4[sl][:, :], out_offset=None,
                        in_=y_scr.ap(),
                        in_offset=bass.IndirectOffsetOnAxis(ap=sidx[:, t:t + 1], axis=0),
                        bounds_check=REGS["sb"], oob_is_err=False),
                        reads=yst_all + [ph + f"sidx{t}"], writes=[ph + f"cg{sl}"])

                def c_add(t):
                    sl = t % 4
                    P.op("dve", lambda e, t=t, sl=sl: e.scalar_tensor_tensor(
                        out=x_sb[:, t * D:(t + 1) * D], in0=ygb4[sl][:, :],
                        scalar=c_all[:, t:t + 1], in1=x_sb[:, t * D:(t + 1) * D],
                        op0=ALU.mult, op1=ALU.add),
                        reads=[ph + f"cg{sl}", f"call{t}", f"xt{t}"], writes=[f"xt{t}"])

                def c_store(t):
                    P.op("qs", lambda e, t=t: e.dma_start(
                        out=dy[t * 128:(t + 1) * 128, :], in_=x_sb[:, t * D:(t + 1) * D]),
                        reads=[f"xt{t}"], writes=[f"out{t}"])

                nop = lambda t: None
                stages_c = [c_gather, nop, nop, c_add]
                if phase == 2:
                    stages_c.append(c_store)
                if phase == 1 and PHASES["attn"]:
                    l2l = [mk_l2_stages(t, t, xlnt[:, 0:8192], xlnt[:, 8192:16384],
                                        wbuf[1]) for t in range(8)]
                    nl = max(len(s) for s in l2l)
                    stages_c += [
                        lambda idx, j=j: l2l[idx][j]() if idx < 8 else None
                        for j in range(nl)]
                sw_pipe(NT, stages_c)
                if phase == 1:
                    return dict(xlnt=xlnt, wbuf=wbuf, foff=foff)

        def attn(soft=False, shared=None):
            As = contextlib.ExitStack()
            with As:
                def asb(name, shape, dt):
                    return As.enter_context(nc.sbuf_tensor(f"{name}_a", shape, dt))
                if shared:
                    foff = shared["foff"]
                    xln2 = shared["xlnt"][:, 0:8192]
                    v_bf = shared["xlnt"][:, 8192:16384]
                    wbig = shared["wbuf"][1]
                    w0 = shared["wbuf"][0]
                    qkb_ap = [w0[:, 0:2048], w0[:, 2048:4096]]
                    PTs = [w0[:, 7168 + i * 384: 7168 + (i + 1) * 384] for i in range(2)]
                    ma = foff["sidx"]
                    assert ma + 52736 <= 229344, f"manual attn region overflow {ma}"
                    s_cos = nc.alloc_sbuf_tensor_at("cos_a", [128, 1024], F32, offset=foff["gb0"])
                    s_sin = nc.alloc_sbuf_tensor_at("sin_a", [128, 1024], F32, offset=foff["zf0"])
                    q_bf = nc.alloc_sbuf_tensor_at("q_a", [128, 8192], BF16, offset=ma)
                    k_bf = nc.alloc_sbuf_tensor_at("k_a", [128, 8192], BF16, offset=ma + 16384)
                    aO = nc.alloc_sbuf_tensor_at("aO_a", [128, 8192], BF16, offset=ma + 32768)
                    def man4(nm, i0):
                        return [nc.alloc_sbuf_tensor_at(f"{nm}_{i}_a", [128, 2], F32,
                                offset=ma + 49152 + 32 * (i0 * 4 + i)) for i in range(4)]
                    rm2, nm2, rs2, rr2 = man4("rm2", 0), man4("nm2", 1), man4("rs2", 2), man4("rr2", 3)
                    Ps_bf = [nc.alloc_sbuf_tensor_at(f"Pbf{i}_a", [128, 384], BF16,
                             offset=ma + 49664 + 768 * i) for i in range(4)]
                else:
                    s_cos = asb("cos", [128, 1024], F32)
                    s_sin = asb("sin", [128, 1024], F32)
                    xln2 = asb("xln2", [128, KC * 1024], BF16)
                    wbig = asb("wbig", [128, 8192], BF16)
                    qkb = [asb(f"qkb{i}", [128, 2048], BF16) for i in range(2)]
                    qkb_ap = [qkb[0][:, :], qkb[1][:, :]]
                    q_bf = asb("q", [128, 8 * 1024], BF16)
                    aO = asb("aO", [128, 8 * 1024], BF16)
                    k_bf = asb("k", [128, 8 * 1024], BF16)
                    v_bf = asb("v", [128, 8 * 1024], BF16)
                    Ps_bf = [asb(f"Pbf{i}", [128, 384], BF16) for i in range(4)]
                    PTs = [asb(f"PT{i}", [128, 384], BF16) for i in range(2)]
                    rm2 = [asb(f"rm2_{i}", [128, 2], F32) for i in range(4)]
                    nm2 = [asb(f"nm2_{i}", [128, 2], F32) for i in range(4)]
                    rs2 = [asb(f"rs2_{i}", [128, 2], F32) for i in range(4)]
                    rr2 = [asb(f"rr2_{i}", [128, 2], F32) for i in range(4)]
                poslots = [(0, 512), (1, 0), (1, 512), (2, 0)]
                ph_gz = (["p1_gb0", "p1_gb1", "p1_zf0", "p1_zf1"] if soft else [])
                ph_wb = ["p1_wbuf0", "p1_wbuf1"] if soft else []

                P.op("dve", lambda e: e.memset(aO[:, :], 0.0),
                    reads=[f"xt{NT - 1}"] if soft else [],
                    writes=[f"aO{c}" for c in range(8)])
                for sg in range(NSG):
                    ph0 = ph_gz if sg == 0 else []
                    if not (shared and sg == 0):
                        P.op("qg", lambda e: e.dma_start(
                            out=wbig.ap().rearrange("p (k f) -> p k f", k=8),
                            in_=dwqkv.rearrange("(k p) c -> p k c", p=128)[:, :, 2048:3072]),
                            writes=["wbig"])
                    P.op("qs", lambda e, sg=sg: e.dma_start(
                        out=s_cos[:, :], in_=dcos[:, sg * 1024:(sg + 1) * 1024]),
                        writes=["cos"] + ph0)
                    P.op("qs", lambda e, sg=sg: e.dma_start(
                        out=s_sin[:, :], in_=dsin[:, sg * 1024:(sg + 1) * 1024]),
                        writes=["sin"])
                    if not (shared and sg == 0):
                        l2_lists = [mk_l2_stages(sg * 8 + ts, ts, xln2, v_bf, wbig)
                                    for ts in range(8)]
                        nst_l = max(len(s) for s in l2_lists)
                        sw_pipe(8, [lambda idx, j=j: l2_lists[idx][j]() if j < len(l2_lists[idx]) else None
                                    for j in range(nst_l)])
                    xl2all = [f"xl2_{ts}" for ts in range(8)]
                    for half, (base, dst, dkey) in enumerate(((0, q_bf, "q"), (8, k_bf, "k"))):
                        for pr in range(4):
                            ocA = base + 2 * pr
                            sl = pr % 2
                            ia, ib = (2 * pr) % 4, (2 * pr + 1) % 4
                            phq = ph_wb if (sg == 0 and half == 0 and pr < 2) else []
                            P.op("qg", lambda e, sl=sl, ocA=ocA: e.dma_start(
                                out=qkb_ap[sl].rearrange("p (k c) -> p k c", k=8),
                                in_=dwqkv.rearrange("(k p) c -> p k c", p=128)[:, :, ocA * 128:(ocA + 2) * 128]),
                                writes=[f"qkb{sl}"] + phq)
                            for j, pidx in ((0, ia), (1, ib)):
                                for k in range(KC):
                                    for os_ in range(2):
                                        P.op("pe", lambda e, j=j, sl=sl, k=k, os_=os_, pidx=pidx: e.matmul(
                                            p4[pidx][:, os_ * 512:(os_ + 1) * 512],
                                            qkb_ap[sl][:, k * 256 + j * 128: k * 256 + j * 128 + 128],
                                            xln2[:, k * 1024 + os_ * 512: k * 1024 + (os_ + 1) * 512],
                                            start=(k == 0), stop=(k == KC - 1)),
                                            reads=[f"qkb{sl}"] + xl2all,
                                            writes=[PB(pidx, os_ * 512)])
                            pA, pB = p4[ia], p4[ib]
                            pAk = [PB(ia, 0), PB(ia, 512)]
                            pBk = [PB(ib, 0), PB(ib, 512)]
                            cE = (ocA - base) * 1024
                            cO = cE + 1024
                            if not ATT_SUB["rope"]:
                                P.op("dve", lambda e, pA=pA, dst=dst, cE=cE: e.tensor_copy(
                                    out=dst[:, cE:cE + 1024], in_=pA[:, :]),
                                    reads=pAk, writes=[f"{dkey}{cE}"])
                                P.op("dve", lambda e, pB=pB, dst=dst, cO=cO: e.tensor_copy(
                                    out=dst[:, cO:cO + 1024], in_=pB[:, :]),
                                    reads=pBk, writes=[f"{dkey}{cO}"])
                                continue
                            P.op("dve", lambda e, pA=pA: e.tensor_tensor(
                                out=scrA[0][:, :], in0=pA[:, :], in1=s_cos[:, :], op=ALU.mult),
                                reads=pAk + ["cos"], writes=["scrA0"])
                            P.op("dve", lambda e, pB=pB: e.tensor_tensor(
                                out=scr_b[:, :], in0=pB[:, :], in1=s_sin[:, :], op=ALU.mult),
                                reads=pBk + ["sin"], writes=["scrb"])
                            P.op("dve", lambda e, dst=dst, cE=cE: e.tensor_tensor(
                                out=dst[:, cE:cE + 1024], in0=scrA[0][:, :], in1=scr_b[:, :],
                                op=ALU.subtract),
                                reads=["scrA0", "scrb"], writes=[f"{dkey}{cE}"])
                            P.op("dve", lambda e, pA=pA: e.tensor_tensor(
                                out=scrA[0][:, :], in0=pA[:, :], in1=s_sin[:, :], op=ALU.mult),
                                reads=pAk + ["sin"], writes=["scrA0"])
                            P.op("dve", lambda e, pB=pB: e.tensor_tensor(
                                out=scr_b[:, :], in0=pB[:, :], in1=s_cos[:, :], op=ALU.mult),
                                reads=pBk + ["cos"], writes=["scrb"])
                            P.op("dve", lambda e, dst=dst, cO=cO: e.tensor_tensor(
                                out=dst[:, cO:cO + 1024], in0=scr_b[:, :], in1=scrA[0][:, :],
                                op=ALU.add),
                                reads=["scrA0", "scrb"], writes=[f"{dkey}{cO}"])
                    ps_slots = [(3, 0), (3, 512), (2, 0), (2, 512)]
                    pt_slots = [(0, 0), (0, 512)]
                    po_slots = [(1, 0), (1, 512)]

                    def stage_qk(b, h):
                        s4 = h % 4
                        pi, pS_off = ps_slots[s4]
                        pS = p4[pi]
                        pSk = PB(pi, pS_off)
                        hb = 32 * (h % 4)
                        cE = 2 * (h // 4) * 1024
                        cO = cE + 1024
                        qr = [f"q{cE}", f"q{cO}"]
                        kr = [f"k{cE}", f"k{cO}"]
                        P.op("pe", lambda e, pS=pS, o=pS_off: e.matmul(
                            pS[:, o:o + 384], s_idbf[:, :], s_mask[:, :],
                            start=True, stop=False),
                            reads=["idbf", "mask"], writes=[pSk])
                        mm = [(0, 128, 0, 0), (0, 128, 1, 0), (1, 256, 0, 128), (1, 256, 1, 128)]
                        for n_, (qt, klen, xh, po) in enumerate(mm):
                            co = cE if xh == 0 else cO
                            P.op("pe", lambda e, pS=pS, o=pS_off, qt=qt, klen=klen, co=co, po=po, hb=hb, b=b, n_=n_: e.matmul(
                                pS[:, o + po:o + po + klen],
                                q_bf[hb:hb + 32, co + b * 256 + qt * 128: co + b * 256 + (qt + 1) * 128],
                                k_bf[hb:hb + 32, co + b * 256: co + b * 256 + klen],
                                start=False, stop=(n_ == 3), tile_position=(hb, 0)),
                                reads=qr + kr, writes=[pSk])

                    def stage_smA(b, h):
                        s4 = h % 4
                        pi, o = ps_slots[s4]
                        pS = p4[pi]
                        pSk = PB(pi, o)
                        P.op("dve", lambda e, pS=pS, o=o, s4=s4: e.reduce_max(
                            out=rm2[s4][:, 0:1], in_=pS[:, o:o + 128], axis=AX.X),
                            reads=[pSk], writes=[f"rm{s4}"])
                        P.op("dve", lambda e, pS=pS, o=o, s4=s4: e.reduce_max(
                            out=rm2[s4][:, 1:2], in_=pS[:, o + 128:o + 384], axis=AX.X),
                            reads=[pSk], writes=[f"rm{s4}"])
                        P.op("dve", lambda e, s4=s4: e.tensor_scalar(
                            out=nm2[s4][:, :], in0=rm2[s4][:, :], scalar1=-0.125,
                            scalar2=None, op0=ALU.mult),
                            reads=[f"rm{s4}"], writes=[f"nm{s4}"])

                    def stage_smB(b, h):
                        s4 = h % 4
                        pi, o = ps_slots[s4]
                        pS = p4[pi]
                        pSk = PB(pi, o)
                        P.op("act", lambda e, pS=pS, o=o, s4=s4: e.activation(
                            out=Ps_bf[s4][:, 0:128], in_=pS[:, o:o + 128], func=AF.Exp,
                            scale=0.125, bias=nm2[s4][:, 0:1], accum_out=rs2[s4][:, 0:1]),
                            reads=[pSk, f"nm{s4}"], writes=[f"Psb{s4}", f"rs{s4}"])
                        P.op("act", lambda e, pS=pS, o=o, s4=s4: e.activation(
                            out=Ps_bf[s4][:, 128:384], in_=pS[:, o + 128:o + 384], func=AF.Exp,
                            scale=0.125, bias=nm2[s4][:, 1:2], accum_out=rs2[s4][:, 1:2]),
                            reads=[pSk, f"nm{s4}"], writes=[f"Psb{s4}", f"rs{s4}"])

                    def stage_smC(b, h):
                        s4 = h % 4
                        P.op("dve", lambda e, s4=s4: e.reciprocal(out=rr2[s4][:, :], in_=rs2[s4][:, :]),
                            reads=[f"rs{s4}"], writes=[f"rr{s4}"])
                        P.op("dve", lambda e, s4=s4: e.tensor_scalar(
                            out=Ps_bf[s4][:, 0:128], in0=Ps_bf[s4][:, 0:128], scalar1=rr2[s4][:, 0:1],
                            scalar2=None, op0=ALU.mult),
                            reads=[f"Psb{s4}", f"rr{s4}"], writes=[f"Pbb{s4}"])
                        P.op("dve", lambda e, s4=s4: e.tensor_scalar(
                            out=Ps_bf[s4][:, 128:384], in0=Ps_bf[s4][:, 128:384], scalar1=rr2[s4][:, 1:2],
                            scalar2=None, op0=ALU.mult),
                            reads=[f"Psb{s4}", f"rr{s4}"], writes=[f"Pbb{s4}"])

                    def stage_smD(b, h):
                        s4 = h % 4
                        s2 = h % 2
                        pti, pToff = pt_slots[s2]
                        pTb = p4[pti][:, pToff:pToff + 256].bitcast(BF16)
                        pTk = PB(pti, pToff)
                        for i3 in range(3):
                            P.op("pe", lambda e, pTb=pTb, i3=i3, s4=s4: e.transpose(
                                pTb[:, i3 * 128:(i3 + 1) * 128],
                                Ps_bf[s4][:, i3 * 128:(i3 + 1) * 128], s_idbf[:, :]),
                                reads=[f"Pbb{s4}", "idbf"], writes=[pTk])
                        P.op("act", lambda e, pTb=pTb, s2=s2: e.copy(
                            out=PTs[s2][:, :], in_=pTb[:, 0:384]),
                            reads=[pTk], writes=[f"PTsb{s2}"])

                    def stage_smE(b, h):
                        s2 = h % 2
                        poi, poff = po_slots[s2]
                        pO = p4[poi]
                        pOk = PB(poi, poff)
                        hp = 64 * (h % 2)
                        P.op("pe", lambda e, pO=pO, poff=poff, hp=hp, b=b, h=h, s2=s2: e.matmul(
                            pO[hp:hp + 64, poff:poff + 256],
                            v_bf[:, (b * 2) * 1024 + 64 * h: (b * 2) * 1024 + 64 * h + 64],
                            PTs[s2][:, 0:256], start=True, stop=False,
                            tile_position=(0, hp)),
                            reads=[f"v{b*2}", f"PTsb{s2}"], writes=[pOk])
                        P.op("pe", lambda e, pO=pO, poff=poff, hp=hp, b=b, h=h, s2=s2: e.matmul(
                            pO[hp:hp + 64, poff + 128:poff + 256],
                            v_bf[:, (b * 2 + 1) * 1024 + 64 * h: (b * 2 + 1) * 1024 + 64 * h + 64],
                            PTs[s2][:, 256:384], start=False, stop=True,
                            tile_position=(0, hp)),
                            reads=[f"v{b*2+1}", f"PTsb{s2}"], writes=[pOk])
                        P.op("act", lambda e, pO=pO, poff=poff, hp=hp, h=h, b=b: e.copy(
                            out=aO[hp:hp + 64, (h // 2) * 1024 + b * 256: (h // 2) * 1024 + (b + 1) * 256],
                            in_=pO[hp:hp + 64, poff:poff + 256]),
                            reads=[pOk], writes=[f"aO{h//2}"])

                    heads = [(b, h) for b in range(4) for h in ATT_SUB["heads"]]
                    if ATT_SUB["core"]:
                        stages = [stage_qk, stage_smA, stage_smB, stage_smC, stage_smD, stage_smE]
                        n = len(heads)
                        for i in range(n + len(stages) - 1):
                            for j in range(len(stages) - 1, -1, -1):
                                idx = i - j
                                if 0 <= idx < n:
                                    stages[j](*heads[idx])
                    P.op("qg", lambda e: e.dma_start(
                        out=wbig.ap().rearrange("p (k f) -> p k f", k=8),
                        in_=dwout.rearrange("(k p) f -> p k f", p=128)),
                        writes=["wbig"])
                    for ts in range(8):
                        t = sg * 8 + ts
                        pw = p4[ts % 2]
                        pwk = [PB(ts % 2, 0), PB(ts % 2, 512)]
                        for k in range(KC):
                            for os_ in range(2):
                                P.op("pe", lambda e, pw=pw, k=k, ts=ts, os_=os_: e.matmul(
                                    pw[:, os_ * 512:(os_ + 1) * 512],
                                    aO[:, k * 1024 + ts * 128: k * 1024 + (ts + 1) * 128],
                                    wbig[:, k * 1024 + os_ * 512: k * 1024 + (os_ + 1) * 512],
                                    start=(k == 0), stop=(k == KC - 1)),
                                    reads=[f"aO{k}", "wbig"],
                                    writes=[pwk[os_]])
                        P.op("dve", lambda e, pw=pw, t=t: e.tensor_tensor(
                            out=x_sb[:, t * D:(t + 1) * D], in0=pw[:, :],
                            in1=x_sb[:, t * D:(t + 1) * D], op=ALU.add),
                            reads=pwk + [f"xt{t}"], writes=[f"xt{t}"])

        sh1 = None
        if PHASES["ffn1"]:
            sh1 = ffn(1)
            if not PHASES["attn"]:
                P.barrier()
        if PHASES["attn"]:
            attn(soft=PHASES["ffn1"], shared=sh1 if PHASES["ffn1"] else None)
            P.barrier()
        if PHASES["ffn2"]:
            ffn(2)
            P.barrier()

        if not PHASES["ffn2"]:
            store_idx = P.op("qs", lambda e: e.dma_start(
                out=dy.rearrange("(t p) f -> p t f", p=128),
                in_=x_sb.ap().rearrange("p (t f) -> p t f", t=NT)),
                reads=[f"xt{t}" for t in range(NT)], writes=["out"])
            P.ops[store_idx]["marked"] = True

        P.finalize()

        def qwait(eobj, waited, qeng, qidx):
            QD = QDEPTHS[qeng]
            s = qsems[qeng][qidx % QD]
            val = 16 * (qidx // QD + 1)
            key = (qeng, qidx % QD)
            if waited.get(key, 0) < val:
                eobj.wait_ge(s, val)
                waited[key] = val

        def P_emit(names, eobj):
            if isinstance(names, str):
                names = (names,)
            waited = {}
            issued = 0
            for o in P.ops:
                name = o["eng"]
                if name not in names:
                    continue
                if name in P.QE and o["sem"] is None:
                    if issued >= QDEPTHS[name]:
                        qwait(eobj, waited, name, issued - QDEPTHS[name])
                    issued += 1
                for d in sorted(o["fdeps"]):
                    dep = P.ops[d]
                    if dep["sem"] is not None:
                        key = id(dep["sem"])
                        if waited.get(key, 0) < 1:
                            eobj.wait_ge(dep["sem"], 1)
                            waited[key] = 1
                        continue
                    deng = dep["eng"]
                    if deng in P.QE:
                        qwait(eobj, waited, deng, dep["qidx"])
                        continue
                    val = dep["mcount"]
                    if val <= 0 or waited.get(deng, 0) >= val:
                        continue
                    eobj.wait_ge(sem[deng], val)
                    waited[deng] = val
                ins = o["fn"](eobj)
                if o["sem"] is not None:
                    ins.then_inc(o["sem"])
                elif o["marked"]:
                    if name in P.QE:
                        ins.then_inc(qsems[name][(issued - 1) % QDEPTHS[name]], 16)
                    else:
                        ins.then_inc(sem[name], 1)

        nst = sum(1 for o in P.ops if o["eng"] == "qs" and o["sem"] is None)

        @block.tensor
        def _(eng):
            P_emit("pe", eng)

        @block.scalar
        def _(eng):
            P_emit(("act", "qa"), eng)

        @block.vector
        def _(eng):
            P_emit("dve", eng)

        @block.sync
        def _(eng):
            P_emit("qs", eng)
            w = {}
            for i in range(min(QDEPTHS["qs"], nst)):
                qwait(eng, w, "qs", nst - 1 - i)

        @block.gpsimd
        def _(eng):
            r_sb = eng.alloc_register()
            REGS["sb"] = r_sb
            eng.reg_mov(REGS["sb"], NSLOT - 1)
            r_gb = eng.alloc_register()
            REGS["gb"] = r_gb
            eng.reg_mov(REGS["gb"], NSLOT)
            P_emit("qg", eng)

    return nc



_PERM = qk_col_perm()


def stage_inputs(L0_attn_x, L0_attn_Wqkv, L0_attn_Wout, L0_ffn_wg, L0_ffn_W, L0_ffn_b):
    x = np.ascontiguousarray(np.asarray(L0_attn_x, dtype=np.float32).reshape(B * L, D))
    wqkvT = np.ascontiguousarray(np.asarray(L0_attn_Wqkv, dtype=np.float32).T[:, _PERM])
    woutT = np.ascontiguousarray(np.asarray(L0_attn_Wout, dtype=np.float32).T)
    wexpT = np.ascontiguousarray(np.asarray(L0_ffn_W, dtype=np.float32).transpose(0, 2, 1))
    wg = np.ascontiguousarray(np.asarray(L0_ffn_wg, dtype=np.float32))
    bias = np.ascontiguousarray(np.asarray(L0_ffn_b, dtype=np.float32))
    ident, ident_bf, mask384, sel_bf = build_consts()
    eidx = np.tile(np.arange(8, dtype=np.float32)[None, :], (128, 1))
    tokbig = (65536.0 + np.arange(NT, dtype=np.float32)[None, :] * 128
              + np.arange(128, dtype=np.float32)[:, None]).astype(np.float32)
    ones1 = np.ones((1, 128), dtype=np.float32)
    zinit = np.zeros((NE * 512, D), dtype=BF16NP)
    maps = []
    for c in range(NCORES):
        cos, sin = rope_tables(c)
        rank = (np.arange(8) < c).astype(np.float32).reshape(8, 1)
        maps.append({
            "x": np.ascontiguousarray(x[c * TT:(c + 1) * TT]),
            "wqkvT": wqkvT, "woutT": woutT, "wexpT": wexpT, "wg": wg, "bias": bias,
            "cos": cos, "sin": sin, "ident": ident, "ident_bf": ident_bf,
            "mask384": mask384, "sel": sel_bf, "rankmask": rank,
            "eidx": eidx, "tokbig": tokbig, "ones1": ones1,
            "z_scr": zinit,
        })
    return maps


def kernel(L0_attn_x, L0_attn_Wqkv, L0_attn_Wout, L0_ffn_wg, L0_ffn_W, L0_ffn_b):
    maps = stage_inputs(L0_attn_x, L0_attn_Wqkv, L0_attn_Wout, L0_ffn_wg, L0_ffn_W, L0_ffn_b)
    nc = build_nc()
    res = run_bass_kernel_spmd(nc, maps, core_ids=list(range(NCORES)))
    out = np.concatenate([np.asarray(r["y"]) for r in res.results], axis=0)
    return out.reshape(B, L, D).astype(np.float32)


# version 33 (final)
# speedup vs baseline: 1.0441x; 1.0441x over previous
import contextlib
import numpy as np
import ml_dtypes
import concourse.bass as bass
import concourse.mybir as mybir
from concourse.bass_utils import run_bass_kernel_spmd

F32 = mybir.dt.float32
BF16 = mybir.dt.bfloat16
AF = mybir.ActivationFunctionType
ALU = mybir.AluOpType
AX = mybir.AxisListType
BF16NP = ml_dtypes.bfloat16

NCORES = 8
B, L, D = 2, 8192, 1024
TT = 2048
NT = 16
KC = 8
NE = 8
CAP = 2048.0
NG = 4
NSG = 2
NH = 16
NEGINF = -3.0e38

PHASES = dict(ffn1=True, attn=True, ffn2=True)
FFN_SUB = dict(route=True, B=True, level=99, nexp=8, bias=True)
ATT_SUB = dict(heads=tuple(range(16)), rope=True, core=True)
DEBUG = False
DEBUG2 = False



def qk_col_perm():
    q = np.zeros(1024, dtype=np.int64)
    for c in range(4):
        for half in range(2):
            for hh in range(4):
                h = 4 * c + hh
                for j in range(32):
                    q[(2 * c + half) * 128 + 32 * hh + j] = 64 * h + 2 * j + half
    perm = np.arange(3072, dtype=np.int64)
    perm[0:1024] = q
    perm[1024:2048] = 1024 + q
    return perm


def rope_tables(core):
    j = (np.arange(128) % 32).astype(np.float32)
    inv = (np.float32(1.0) / np.float32(10000.0) ** ((2.0 * j).astype(np.float32) / np.float32(64.0))).astype(np.float32)
    pos = ((core % 4) * TT + np.arange(TT)).astype(np.float32)
    t = pos[None, :].astype(np.float32) * inv[:, None]
    return np.cos(t).astype(np.float32), np.sin(t).astype(np.float32)


def build_consts():
    ident = np.eye(128, dtype=np.float32)
    ident_bf = np.eye(128, dtype=BF16NP)
    r = np.arange(128)
    tri = np.where(r[None, :] <= r[:, None], 0.0, NEGINF).astype(np.float32)
    m = np.zeros((128, 384), dtype=np.float32)
    m[:, 0:128] = tri
    m[:, 256:384] = tri
    mask384 = m.astype(BF16NP)
    sel = np.zeros((8, 1024), dtype=np.float32)
    for e in range(8):
        sel[e, e * 128:(e + 1) * 128] = 1.0
    return ident, ident_bf, mask384, sel.astype(BF16NP)



class Prog:
    CE = ("pe", "act", "dve")
    QE = ("qs", "qg", "qa")

    def __init__(self):
        self.ops = []
        self.last_w = {}
        self.readers = {}
        self.pending_barrier = {}

    def op(self, eng, fn, reads=(), writes=(), sem=None):
        idx = len(self.ops)
        deps = set()
        for rsc in reads:
            w = self.last_w.get(rsc)
            if w is not None:
                deps.add(w)
            if rsc.startswith("pb"):
                for rd in self.readers.get(rsc, ()):
                    if self.ops[rd]["eng"] != eng:
                        deps.add(rd)
        for rsc in writes:
            w = self.last_w.get(rsc)
            if w is not None:
                deps.add(w)
            deps.update(self.readers.get(rsc, ()))
        if eng in self.pending_barrier:
            deps |= self.pending_barrier.pop(eng)
        self.ops.append(dict(idx=idx, eng=eng, fn=fn, deps=deps, sem=sem, marked=False))
        for rsc in reads:
            self.readers.setdefault(rsc, []).append(idx)
        for rsc in writes:
            self.last_w[rsc] = idx
            self.readers[rsc] = []
        return idx

    def barrier(self):
        last = {}
        for o in self.ops:
            last[o["eng"]] = o["idx"]
        deps = set(last.values())
        for eng in list(self.CE) + list(self.QE):
            self.pending_barrier.setdefault(eng, set()).update(deps)

    def finalize(self):
        for o in self.ops:
            if o["eng"] in self.QE:
                o["marked"] = True
            best = {}
            qdeps = set()
            for d in o["deps"]:
                dep = self.ops[d]
                if dep["eng"] == o["eng"] and dep["sem"] is None and o["eng"] == "pe":
                    continue
                if dep["sem"] is not None or dep["eng"] in self.QE:
                    qdeps.add(d)
                else:
                    k = dep["eng"]
                    if k not in best or best[k] < d:
                        best[k] = d
            fd = qdeps | set(best.values())
            o["fdeps"] = fd
            for d in fd:
                self.ops[d]["marked"] = True
        cnt = {}
        for o in self.ops:
            if o["eng"] in self.QE:
                if o["sem"] is None:
                    o["qidx"] = cnt.get(o["eng"], 0)
                    cnt[o["eng"]] = o["qidx"] + 1
                else:
                    o["qidx"] = None
                o["mcount"] = cnt.get(o["eng"], 0)
            else:
                if o["marked"]:
                    cnt[o["eng"]] = cnt.get(o["eng"], 0) + 1
                o["mcount"] = cnt.get(o["eng"], 0)


def build_nc():
    nc = bass.Bass("TRN2", target_bir_lowering=False)

    dx = nc.dram_tensor("x", [TT, D], F32, kind="ExternalInput").ap()
    dwqkv = nc.dram_tensor("wqkvT", [D, 3072], F32, kind="ExternalInput").ap()
    dwout = nc.dram_tensor("woutT", [D, D], F32, kind="ExternalInput").ap()
    dwexp = nc.dram_tensor("wexpT", [NE, D, D], F32, kind="ExternalInput").ap()
    dwg = nc.dram_tensor("wg", [D, NE], F32, kind="ExternalInput").ap()
    dbias = nc.dram_tensor("bias", [NE, D], F32, kind="ExternalInput").ap()
    dcos = nc.dram_tensor("cos", [128, TT], F32, kind="ExternalInput").ap()
    dsin = nc.dram_tensor("sin", [128, TT], F32, kind="ExternalInput").ap()
    dident = nc.dram_tensor("ident", [128, 128], F32, kind="ExternalInput").ap()
    didbf = nc.dram_tensor("ident_bf", [128, 128], BF16, kind="ExternalInput").ap()
    dmask = nc.dram_tensor("mask384", [128, 384], BF16, kind="ExternalInput").ap()
    dsel = nc.dram_tensor("sel", [8, 1024], BF16, kind="ExternalInput").ap()
    deidx = nc.dram_tensor("eidx", [128, 8], F32, kind="ExternalInput").ap()
    dtokbig = nc.dram_tensor("tokbig", [128, NT], F32, kind="ExternalInput").ap()
    dones1 = nc.dram_tensor("ones1", [1, 128], F32, kind="ExternalInput").ap()
    drank = nc.dram_tensor("rankmask", [8, 1], F32, kind="ExternalInput").ap()
    dy = nc.dram_tensor("y", [TT, D], F32, kind="ExternalOutput").ap()

    KCAP = 512
    NSLOT = NE * KCAP
    z_scr = nc.dram_tensor("z_scr", [NSLOT, D], BF16, kind="ExternalInput")
    y_scr = nc.dram_tensor("y_scr", [NSLOT, D], F32)
    if DEBUG:
        dbg_logit = nc.dram_tensor("dbg_logit", [128, NT * 8], F32, kind="ExternalOutput").ap()
        dbg_msk = nc.dram_tensor("dbg_msk", [128, NT * 8], F32, kind="ExternalOutput").ap()
        dbg_scan = nc.dram_tensor("dbg_scan", [8, TT], F32, kind="ExternalOutput").ap()
        dbg_cT = nc.dram_tensor("dbg_cT", [8, TT], BF16, kind="ExternalOutput").ap()
        dbg_gate = nc.dram_tensor("dbg_gate", [128, NT], F32, kind="ExternalOutput").ap()
        dbg_cb = nc.dram_tensor("dbg_cb", [128, NE * 512], BF16, kind="ExternalOutput").ap()
        dbg_xln = nc.dram_tensor("dbg_xln", [128, KC * TT], BF16, kind="ExternalOutput").ap()
        dbg_coff = nc.dram_tensor("dbg_coff", [8, 1], F32, kind="ExternalOutput").ap()
        dbg_wb = nc.dram_tensor("dbg_wb", [128, 8192], BF16, kind="ExternalOutput").ap()
        dbg_z = nc.dram_tensor("dbg_z", [128, KC * 512], BF16, kind="ExternalOutput").ap()
        dbg_ps = nc.dram_tensor("dbg_ps", [128, 1024], F32, kind="ExternalOutput").ap()
    if DEBUG2:
        dbg_sidx = nc.dram_tensor("dbg_sidx", [128, NT], F32, kind="ExternalOutput").ap()
        dbg_gidx = nc.dram_tensor("dbg_gidx", [128, NT], F32, kind="ExternalOutput").ap()
        dbg_xlnc = nc.dram_tensor("dbg_xlnc", [128, NT * D], BF16, kind="ExternalOutput").ap()
    cc_in = [nc.dram_tensor(f"cc_in{i}", [1, 8], F32) for i in range(2)]
    cc_out = [nc.dram_tensor(f"cc_out{i}", [8, 8], F32) for i in range(2)]

    P = Prog()
    REGS = {}
    es = contextlib.ExitStack()
    with es:
        def sb(name, shape, dt):
            return es.enter_context(nc.sbuf_tensor(name, shape, dt))

        x_sb = sb("x_sb", [128, NT * D], F32)
        s_id = sb("s_id", [128, 128], F32)
        s_idbf = sb("s_idbf", [128, 128], BF16)
        s_mask = sb("s_mask", [128, 384], BF16)
        s_wg = sb("s_wg", [128, 64], F32)
        s_bias = sb("s_bias", [8, D], BF16)
        s_rank = sb("s_rank", [8, 1], F32)
        s_eps = sb("s_eps", [128, 1], F32)
        scrA = [sb(f"scrA{i}", [128, D], F32) for i in range(2)]
        scr_b = sb("scr_b", [128, D], F32)
        c_ssum = sb("c_ssum", [128, NT], F32)
        c_sumsq = sb("c_sumsq", [128, NT], F32)
        c_nmean = sb("c_nmean", [128, NT], F32)
        c_rstd = sb("c_rstd", [128, NT], F32)
        c_var = sb("c_var", [128, 4], F32)
        c_gate = sb("c_gate", [128, NT], F32)
        c_tmp = sb("c_tmp", [128, 4], F32)
        c_rt = sb("c_rt", [128, 4], F32)
        c_posel = sb("c_posel", [128, 2], F32)
        c_cc = sb("c_cc", [128, 4], F32)
        s_pos8 = sb("s_pos8", [128, 16], F32)
        s_cnt = sb("s_cnt", [8, 1], F32)
        s_gath = sb("s_gath", [8, 8], F32)
        s_coff = sb("s_coff", [8, 1], F32)
        s_eidx = sb("s_eidx", [128, 8], F32)
        s_ones1 = sb("s_ones1", [1, 128], F32)
        s_c4096 = sb("s_c4096", [128, 1], F32)
        s_coffb = sb("s_coffb", [128, 8], F32)
        s_crow = sb("s_crow", [1, 8], F32)
        c_e2 = sb("c_e2", [128, 2], F32)
        c_poselN = sb("c_poselN", [128, NT], F32)
        c_all = sb("c_all", [128, NT], F32)
        c_lp = sb("c_lp", [128, 2], F32)
        c_ov = sb("c_ov", [128, 2], F32)
        c_sl = sb("c_sl", [128, 2], F32)

        p4 = [es.enter_context(nc.psum_tensor(f"p4_{i}", [128, 1024], F32)) for i in range(4)]
        PB = lambda i, off: f"pb{2 * i + (1 if off >= 512 else 0)}"

        sem = {n: es.enter_context(nc.semaphore(f"sem_{n}")) for n in ("pe", "act", "dve")}
        QDEPTHS = {"qs": 6, "qg": 8, "qa": 2}
        qsems = {q: [es.enter_context(nc.semaphore(f"sem_{q}{i}")) for i in range(QDEPTHS[q])]
                 for q in ("qs", "qg", "qa")}
        cc_sems = [es.enter_context(nc.semaphore(f"cc{i}")) for i in range(2)]
        block = es.enter_context(nc.Block())

        for t in range(NT):
            P.op("qs", lambda e, t=t: e.dma_start(
                out=x_sb[:, t * D:(t + 1) * D], in_=dx[t * 128:(t + 1) * 128, :]),
                writes=[f"xt{t}"])
        for dst, src, key in ((s_id, dident, "id"), (s_idbf, didbf, "idbf"),
                              (s_mask, dmask, "mask"),
                              (s_rank, drank, "rank")):
            P.op("qs", lambda e, d=dst, s=src: e.dma_start(out=d[:, :], in_=s[:, :]), writes=[key])
        P.op("qs", lambda e: e.dma_start(out=s_wg.ap().rearrange("p (k e) -> p k e", k=8),
                                         in_=dwg.rearrange("(k p) e -> p k e", p=128)), writes=["wg"])
        P.op("qg", lambda e: e.dma_start(out=s_bias[:, :], in_=dbias[:, :]),
            writes=[f"biasrow{e_}" for e_ in range(NE)])
        P.op("dve", lambda e: e.memset(s_eps[:, :], 1e-5), writes=["eps"])
        P.op("dve", lambda e: e.memset(s_c4096[:, :], float(NSLOT)), writes=["c4096"])
        P.op("qs", lambda e: e.dma_start(out=s_eidx[:, :], in_=deidx[:, :]), writes=["eidx"])
        P.op("qs", lambda e: e.dma_start(out=s_ones1[:, :], in_=dones1[:, :]), writes=["ones1"])

        def sw_pipe(n_items, stages):
            for i in range(n_items + len(stages) - 1):
                for j in range(len(stages) - 1, -1, -1):
                    idx = i - j
                    if 0 <= idx < n_items:
                        stages[j](idx)

        def ln_stages(t):
            xk = f"xt{t}"
            q4 = t % 4
            par = t % 2
            def st0(_=None):
                P.op("act", lambda e, t=t: e.activation(
                    out=scr_b[:, :], in_=x_sb[:, t * D:(t + 1) * D], func=AF.Square,
                    accum_out=c_sumsq[:, t:t + 1]),
                    reads=[xk], writes=["scrb", f"sumsq{t}"])
                P.op("act", lambda e, t=t: e.activation(
                    out=scr_b[:, :], in_=x_sb[:, t * D:(t + 1) * D], func=AF.Copy,
                    bias=0.0, accum_out=c_ssum[:, t:t + 1]),
                    reads=[xk], writes=["scrb", f"ssum{t}"])
            def st1(_=None):
                P.op("dve", lambda e, t=t: e.tensor_scalar(
                    out=c_nmean[:, t:t + 1], in0=c_ssum[:, t:t + 1],
                    scalar1=-1.0 / D, scalar2=None, op0=ALU.mult),
                    reads=[f"ssum{t}"], writes=[f"nmean{t}"])
                P.op("dve", lambda e, t=t, q4=q4: e.tensor_tensor(
                    out=c_var[:, q4:q4 + 1], in0=c_nmean[:, t:t + 1],
                    in1=c_nmean[:, t:t + 1], op=ALU.mult),
                    reads=[f"nmean{t}"], writes=[f"var{q4}"])
                P.op("dve", lambda e, t=t, q4=q4: e.scalar_tensor_tensor(
                    out=c_var[:, q4:q4 + 1], in0=c_sumsq[:, t:t + 1], scalar=1.0 / D,
                    in1=c_var[:, q4:q4 + 1], op0=ALU.mult, op1=ALU.subtract),
                    reads=[f"sumsq{t}", f"var{q4}"], writes=[f"var{q4}"])
            def st2(_=None):
                P.op("act", lambda e, q4=q4: e.activation(
                    out=c_tmp[:, q4:q4 + 1], in_=c_var[:, q4:q4 + 1], func=AF.Ln,
                    bias=s_eps[:, :]),
                    reads=[f"var{q4}", "eps"], writes=[f"ctmp{q4}"])
                P.op("act", lambda e, t=t, q4=q4: e.activation(
                    out=c_rstd[:, t:t + 1], in_=c_tmp[:, q4:q4 + 1], func=AF.Exp, scale=-0.5),
                    reads=[f"ctmp{q4}"], writes=[f"rstd{t}"])
            def st3(_=None):
                P.op("dve", lambda e, t=t, par=par: e.tensor_scalar(
                    out=scrA[par][:, :], in0=x_sb[:, t * D:(t + 1) * D],
                    scalar1=c_nmean[:, t:t + 1], scalar2=c_rstd[:, t:t + 1],
                    op0=ALU.add, op1=ALU.mult),
                    reads=[xk, f"nmean{t}", f"rstd{t}"], writes=[f"scrA{par}"])
            return [st0, st1, st2, st3]

        def mk_l2_stages(t, ts, xln2a, v_a, wbig_t):
            par = t % 2
            sts = list(ln_stages(t))

            def s_tp(_=None):
                pa = p4[par]
                pak = [PB(par, 0), PB(par, 512)]
                for k in range(KC):
                    P.op("pe", lambda e, k=k, pa=pa, par=par: e.transpose(
                        pa[:, k * 128:(k + 1) * 128], scrA[par][:, k * 128:(k + 1) * 128],
                        s_id[:, :]),
                        reads=[f"scrA{par}", "id"], writes=pak)

            def s_cp(_=None):
                pa = p4[par]
                pak = [PB(par, 0), PB(par, 512)]
                for k in range(KC):
                    P.op("dve", lambda e, ts=ts, k=k, pa=pa: e.tensor_copy(
                        out=xln2a[:, k * 1024 + ts * 128: k * 1024 + (ts + 1) * 128],
                        in_=pa[:, k * 128:(k + 1) * 128]),
                        reads=pak, writes=[f"xl2_{ts}"])

            def s_v(_=None):
                pv = p4[2 + ts % 2]
                pvk = [PB(2 + ts % 2, 0), PB(2 + ts % 2, 512)]
                for k in range(KC):
                    for os_ in range(2):
                        P.op("pe", lambda e, ts=ts, k=k, os_=os_, pv=pv: e.matmul(
                            pv[:, os_ * 512:(os_ + 1) * 512],
                            xln2a[:, k * 1024 + ts * 128: k * 1024 + (ts + 1) * 128],
                            wbig_t[:, k * 1024 + os_ * 512: k * 1024 + (os_ + 1) * 512],
                            start=(k == 0), stop=(k == KC - 1)),
                            reads=[f"xl2_{ts}", "wbig"],
                            writes=[pvk[os_]])

            def s_ve(_=None):
                pv = p4[2 + ts % 2]
                pvk = [PB(2 + ts % 2, 0), PB(2 + ts % 2, 512)]
                P.op("act", lambda e, ts=ts, pv=pv: e.copy(
                    out=v_a[:, ts * 1024:(ts + 1) * 1024], in_=pv[:, :]),
                    reads=pvk, writes=[f"v{ts}"])
            return sts + [s_tp, s_cp, s_v, s_ve]

        def ffn(phase):
            fs = contextlib.ExitStack()
            foff = {}
            with fs:
                def fsb(name, shape, dt):
                    foff[name] = nc.sbuf_base
                    return fs.enter_context(nc.sbuf_tensor(f"{name}_f{phase}", shape, dt))
                xlnt = fsb("xlnt", [128, NT * D], BF16)
                wbuf = [fsb(f"wb{i}", [128, 8192], BF16) for i in range(2)]
                gbuf = [fsb(f"gb{i}", [128, D], BF16) for i in range(2)]
                zfm = [fsb(f"zf{i}", [128, D], BF16) for i in range(2)]
                sidx = fsb("sidx", [128, NT], mybir.dt.uint32)
                s_logit = fsb("logit", [128, NT * 8], F32)
                s_msk = fsb("msk", [128, NT * 8], F32)
                ybuf = [fsb(f"yb{i}", [128, D], F32) for i in range(2)]
                cgb = [fsb(f"cgb{i}", [128, D], F32) for i in range(2)]
                maskT = fsb("maskT", [8, TT], F32)
                scanT = fsb("scanT", [8, TT], F32)
                xfm32 = [fsb(f"xfm32_{i}", [128, D], F32) for i in range(2)]
                ones8 = fsb("ones8", [8, 128], F32)
                s_sel = fsb("s_sel", [8, 1024], BF16)
                ci = phase - 1
                ph = f"p{phase}_"

                P.op("dve", lambda e: e.memset(ones8[:, :], 1.0), writes=["ones8g"])
                P.op("qs", lambda e: e.dma_start(out=s_sel[:, :], in_=dsel[:, :]),
                    writes=["selc"])

                def load_expert(e_):
                    P.op("qg", lambda e, ei=e_: e.dma_start(
                        out=wbuf[(ei + 1) % 2][:, :].rearrange("p (k f) -> p k f", k=8),
                        in_=dwexp[ei].rearrange("(k p) f -> p k f", p=128)),
                        writes=[ph + f"wbuf{(e_ + 1) % 2}"])

                def a_stages(t):
                    par = t % 2
                    q4 = t % 4
                    lgo = par * 512
                    lg = s_logit[:, t * 8:(t + 1) * 8]
                    sts = list(ln_stages(t))
                    def s_tp(_=None):
                        pa = p4[par]
                        pak = [PB(par, 0), PB(par, 512)]
                        for k in range(KC):
                            P.op("pe", lambda e, k=k, pa=pa, par=par: e.transpose(
                                pa[:, k * 128:(k + 1) * 128], scrA[par][:, k * 128:(k + 1) * 128],
                                s_id[:, :]),
                                reads=[f"scrA{par}", "id"], writes=pak)
                    def s_ev(_=None):
                        pa = p4[par]
                        pak = [PB(par, 0), PB(par, 512)]
                        P.op("act", lambda e, pa=pa, par=par: e.copy(out=xfm32[par][:, :], in_=pa[:, :]),
                            reads=pak, writes=[ph + f"xfm32{par}"])
                    def s_cl(_=None):
                        for k in range(KC):
                            P.op("pe", lambda e, k=k, lgo=lgo, par=par: e.matmul(
                                p4[2][:, lgo:lgo + 8], xfm32[par][:, k * 128:(k + 1) * 128],
                                s_wg[:, k * 8:(k + 1) * 8], start=(k == 0), stop=(k == KC - 1)),
                                reads=[ph + f"xfm32{par}", "wg"],
                                writes=[PB(2, lgo), ph + f"lg{par}"])
                    def s_le(_=None):
                        P.op("act", lambda e, t=t, lgo=lgo: e.copy(
                            out=s_logit[:, t * 8:(t + 1) * 8], in_=p4[2][:, lgo:lgo + 8]),
                            reads=[PB(2, lgo), ph + f"lg{par}"], writes=[ph + f"lgt{t}"])
                    def s_rt(_=None):
                        P.op("dve", lambda e, lg=lg, t=t, q4=q4: e.reduce_max(
                            out=c_rt[:, q4:q4 + 1], in_=lg, axis=AX.X),
                            reads=[ph + f"lgt{t}"], writes=[f"crt{q4}"])
                        P.op("dve", lambda e, lg=lg, t=t, q4=q4: e.tensor_scalar(
                            out=s_msk[:, t * 8:(t + 1) * 8], in0=lg,
                            scalar1=c_rt[:, q4:q4 + 1], scalar2=None, op0=ALU.is_equal),
                            reads=[ph + f"lgt{t}", f"crt{q4}"], writes=[ph + f"msk{t}"])
                        P.op("dve", lambda e, q4=q4: e.tensor_scalar(
                            out=c_cc[:, q4:q4 + 1], in0=c_rt[:, q4:q4 + 1],
                            scalar1=-1.0, scalar2=float(np.log(2.0)), op0=ALU.mult,
                            op1=ALU.add),
                            reads=[f"crt{q4}"], writes=[f"ccc{q4}"])
                    def s_ex(_=None):
                        P.op("act", lambda e, lg=lg, t=t, q4=q4, par=par: e.activation(
                            out=s_pos8[:, par * 8:par * 8 + 8], in_=lg, func=AF.Exp,
                            bias=c_cc[:, q4:q4 + 1], accum_out=c_rt[:, q4:q4 + 1]),
                            reads=[ph + f"lgt{t}", f"ccc{q4}"], writes=[f"pos8{par}", f"crt{q4}"])
                    def s_gt(_=None):
                        P.op("dve", lambda e, t=t, q4=q4: e.reciprocal(
                            out=c_gate[:, t:t + 1], in_=c_rt[:, q4:q4 + 1]),
                            reads=[f"crt{q4}"], writes=[f"gate{t}"])
                    def s_mt(_=None):
                        mto = par * 512
                        P.op("pe", lambda e, t=t, mto=mto: e.transpose(
                            p4[3][0:8, mto:mto + 128], s_msk[:, t * 8:(t + 1) * 8], s_id[:, :]),
                            reads=[ph + f"msk{t}", "id"], writes=[PB(3, mto)])
                        P.op("act", lambda e, t=t, mto=mto: e.copy(
                            out=maskT[:, t * 128:(t + 1) * 128], in_=p4[3][0:8, mto:mto + 128]),
                            reads=[PB(3, mto)], writes=[ph + f"maskT{t}"])
                    orig3 = sts[3]
                    def st3b(_=None):
                        orig3()
                        P.op("dve", lambda e, t=t, par=par: e.tensor_copy(
                            out=xlnt[:, t * D:(t + 1) * D], in_=scrA[par][:, :]),
                            reads=[f"scrA{par}"], writes=[ph + f"xlnt{t}"])
                    sts[3] = st3b
                    lvl = FFN_SUB["level"]
                    if lvl < -1.5:
                        return sts
                    sts += [s_tp]
                    if lvl >= -0.5:
                        sts += [s_ev]
                    if lvl >= 0:
                        sts += [s_cl]
                    if lvl >= 2:
                        sts += [s_le, s_rt, s_ex, s_gt, s_mt]
                    return sts

                def z0(t):
                    par = t % 2
                    if par == 0:
                        P.op("pe", lambda e, t=t: e.transpose(
                            p4[2][:, 544:552], scanT[:, t * 128:(t + 1) * 128], s_id[0:8, 0:8]),
                            reads=[ph + f"scanT{t}", "id"], writes=[PB(2, 512), ph + "posE"])
                    else:
                        P.op("pe", lambda e, t=t: e.transpose(
                            p4[3][:, 256:264], scanT[:, t * 128:(t + 1) * 128], s_id[0:8, 0:8]),
                            reads=[ph + f"scanT{t}", "id"], writes=[PB(3, 0), ph + "posO"])

                def z1(t):
                    par = t % 2
                    pos_ap = p4[2][:, 544:552] if par == 0 else p4[3][:, 256:264]
                    posk = [ph + "posE", PB(2, 512)] if par == 0 else [ph + "posO", PB(3, 0)]
                    P.op("dve", lambda e, t=t, par=par, pos_ap=pos_ap: e.scalar_tensor_tensor(
                        out=s_pos8[:, par * 8:par * 8 + 8], in0=pos_ap, scalar=1.0,
                        in1=s_msk[:, t * 8:(t + 1) * 8], op0=ALU.mult, op1=ALU.mult,
                        accum_out=c_poselN[:, t:t + 1]),
                        reads=posk + [ph + f"msk{t}"], writes=[f"pos8{par}", f"poseln{t}"])
                    P.op("dve", lambda e, t=t, par=par: e.scalar_tensor_tensor(
                        out=s_pos8[:, par * 8:par * 8 + 8], in0=s_msk[:, t * 8:(t + 1) * 8],
                        scalar=1.0, in1=s_eidx[:, :], op0=ALU.mult, op1=ALU.mult,
                        accum_out=c_e2[:, par:par + 1]),
                        reads=[ph + f"msk{t}", "eidx"], writes=[f"pos8{par}", f"ce2{par}"])
                    P.op("dve", lambda e, t=t, par=par: e.tensor_scalar(
                        out=c_sl[:, par:par + 1], in0=c_poselN[:, t:t + 1],
                        scalar1=8.0, scalar2=c_e2[:, par:par + 1],
                        op0=ALU.mult, op1=ALU.add),
                        reads=[f"ce2{par}", f"poseln{t}"], writes=[f"csl{par}"])
                    P.op("dve", lambda e, t=t, par=par: e.tensor_copy(
                        out=sidx[:, t:t + 1], in_=c_sl[:, par:par + 1]),
                        reads=[f"csl{par}"], writes=[ph + f"sidx{t}"])

                def z4(t):
                    P.op("qg", lambda e, t=t: e.indirect_dma_start(
                        out=z_scr.ap(),
                        out_offset=bass.IndirectOffsetOnAxis(ap=sidx[:, t:t + 1], axis=0),
                        in_=xlnt[:, t * D:(t + 1) * D], in_offset=None,
                        bounds_check=REGS["sb"], oob_is_err=False),
                        reads=[ph + f"xlnt{t}", ph + f"sidx{t}"], writes=[ph + f"zsc{t}"])
                    if t == 1 or t == 3:
                        load_expert(t // 2)
                def s_scan(t):
                    init = 0.0 if t == 0 else scanT[:, t * 128 - 1: t * 128]
                    rd = ["ones8g", ph + f"maskT{t}"] + ([] if t == 0 else [ph + f"scanT{t-1}"])
                    P.op("dve", lambda e, t=t, init=init: e.tensor_tensor_scan(
                        out=scanT[:, t * 128:(t + 1) * 128], data0=ones8[:, 0:128],
                        data1=maskT[:, t * 128:(t + 1) * 128],
                        initial=init, op0=ALU.mult, op1=ALU.add),
                        reads=rd, writes=[ph + f"scanT{t}"])
                stage_lists = [a_stages(t) for t in range(NT)]
                nst_a = max(len(s) for s in stage_lists)
                stages_a = [lambda idx, j=j: stage_lists[idx][j]() if j < len(stage_lists[idx]) else None
                            for j in range(nst_a)]
                if FFN_SUB["level"] >= 2 and FFN_SUB["route"]:
                    stages_a = stages_a + [s_scan, z0, z1, z4]
                sw_pipe(NT, stages_a)
                if FFN_SUB["level"] < 2:
                    return
                if not FFN_SUB["route"]:
                    return
                P.op("act", lambda e: e.copy(out=s_cnt[:, :], in_=scanT[:, TT - 1:TT]),
                    reads=[ph + f"scanT{NT-1}"], writes=["cnt"])
                P.op("qa", lambda e, ci=ci: e.dma_start(
                    out=cc_in[ci].ap().rearrange("a e -> e a"), in_=s_cnt[:, :]),
                    reads=["cnt"], writes=[f"ccin{ci}"])
                P.op("qg", lambda e, ci=ci: e.collective_compute(
                    "AllGather", ALU.bypass, replica_groups=[list(range(NCORES))],
                    ins=[cc_in[ci].ap().opt()], outs=[cc_out[ci].ap().opt()]),
                    reads=[f"ccin{ci}"], writes=[f"ccout{ci}"], sem=cc_sems[ci])

                if DEBUG2 and phase == 1:
                    P.op("qg", lambda e: e.dma_start(out=dbg_sidx[:, :], in_=sidx[:, :]),
                        reads=[ph + f"sidx{t2}" for t2 in range(NT)], writes=["dd1"])
                    P.op("qs", lambda e: e.dma_start(out=dbg_xlnc[:, :], in_=xlnt[:, :]),
                        reads=[ph + f"xlnt{t2}" for t2 in range(NT)], writes=["dd3"])

                if not FFN_SUB["B"]:
                    return
                SPE = KCAP // 128
                NST = NE * SPE
                zsc_all = [ph + f"zsc{t}" for t in range(NT)]

                zr8 = z_scr.ap().rearrange("(p e) f -> e p f", e=8)
                yr8 = y_scr.ap()[0:NSLOT, :].rearrange("(p e) f -> e p f", e=8)

                def b_load(st):
                    sl = st % 2
                    e_, i_ = st // SPE, st % SPE
                    P.op("qs", lambda e, e_=e_, i_=i_, sl=sl: e.dma_start(
                        out=gbuf[sl][:, :], in_=zr8[e_][i_ * 128:(i_ + 1) * 128, :]),
                        reads=zsc_all, writes=[ph + f"gb{sl}"])

                def b_tp(st):
                    sl = st % 2
                    pbf = p4[st % 2][:, 0:512].bitcast(BF16)
                    pak = [PB(st % 2, 0)]
                    for k in range(KC):
                        P.op("pe", lambda e, k=k, sl=sl, pbf=pbf: e.transpose(
                            pbf[:, k * 128:(k + 1) * 128], gbuf[sl][:, k * 128:(k + 1) * 128],
                            s_idbf[:, :]),
                            reads=[ph + f"gb{sl}", "idbf"], writes=pak)
                    P.op("act", lambda e, sl=sl, pbf=pbf: e.copy(out=zfm[sl][:, :], in_=pbf[:, :]),
                        reads=pak, writes=[ph + f"zf{sl}"])

                def b_mm(st):
                    sl = st % 2
                    e_ = st // SPE
                    pm = p4[2 + st % 2]
                    for k in range(KC):
                        for os_ in range(2):
                            P.op("pe", lambda e, k=k, os_=os_, sl=sl, pm=pm: e.matmul(
                                pm[:, os_ * 512:(os_ + 1) * 512],
                                zfm[sl][:, k * 128:(k + 1) * 128],
                                wbuf[(e_ + 1) % 2][:, k * 1024 + os_ * 512: k * 1024 + (os_ + 1) * 512],
                                start=(k == 0), stop=False),
                                reads=[ph + f"zf{sl}", ph + f"wbuf{(e_ + 1) % 2}"],
                                writes=[PB(2 + st % 2, os_ * 512)])
                    for os_ in range(2):
                        P.op("pe", lambda e, os_=os_, e_=e_, pm=pm: e.matmul(
                            pm[:, os_ * 512:(os_ + 1) * 512],
                            s_sel[0:8, e_ * 128:(e_ + 1) * 128],
                            s_bias[0:8, os_ * 512:(os_ + 1) * 512],
                            start=False, stop=True),
                            reads=["selc", f"biasrow{e_}"],
                            writes=[PB(2 + st % 2, os_ * 512)])
                    if st % SPE == SPE - 1 and e_ + 2 < NE:
                        load_expert(e_ + 2)

                def b_ev(st):
                    sl = st % 2
                    pm = p4[2 + st % 2]
                    P.op("act", lambda e, sl=sl, pm=pm: e.copy(out=ybuf[sl][:, :], in_=pm[:, :]),
                        reads=[PB(2 + st % 2, 0), PB(2 + st % 2, 512)], writes=[ph + f"yb{sl}"])

                def b_st(st):
                    sl = st % 2
                    e_, i_ = st // SPE, st % SPE
                    P.op("qs", lambda e, e_=e_, i_=i_, sl=sl, st=st: e.dma_start(
                        out=yr8[e_][i_ * 128:(i_ + 1) * 128, :], in_=ybuf[sl][:, :]),
                        reads=[ph + f"yb{sl}"], writes=[ph + f"yst{st}"])

                def y1(t):
                    par = t % 2
                    P.op("dve", lambda e, t=t, par=par: e.scalar_tensor_tensor(
                        out=s_pos8[:, par * 8:par * 8 + 8], in0=s_msk[:, t * 8:(t + 1) * 8],
                        scalar=1.0, in1=s_coffb[:, :], op0=ALU.mult, op1=ALU.mult,
                        accum_out=c_lp[:, par:par + 1]),
                        reads=[ph + f"msk{t}", "coffb"], writes=[f"pos8{par}", f"clp{par}"])
                    P.op("dve", lambda e, t=t, par=par: e.tensor_tensor(
                        out=c_lp[:, par:par + 1], in0=c_poselN[:, t:t + 1],
                        in1=c_lp[:, par:par + 1], op=ALU.add),
                        reads=[f"poseln{t}", f"clp{par}"], writes=[f"clp{par}"])
                    P.op("dve", lambda e, t=t, par=par: e.scalar_tensor_tensor(
                        out=c_all[:, t:t + 1], in0=c_lp[:, par:par + 1], scalar=CAP,
                        in1=c_gate[:, t:t + 1], op0=ALU.is_lt, op1=ALU.mult),
                        reads=[f"clp{par}", f"gate{t}"], writes=[f"call{t}"])

                def emit_coff_chain():
                    P.op("qa", lambda e, ci=ci: e.dma_start(out=s_gath[:, :], in_=cc_out[ci][:, :]),
                        reads=[f"ccout{ci}"], writes=["gath"])
                    P.op("pe", lambda e: e.matmul(p4[0][0:8, 512:513], s_gath[:, :], s_rank[:, :],
                                                  start=True, stop=True),
                        reads=["gath", "rank"], writes=[PB(0, 512), ph + "coffR"])
                    P.op("act", lambda e: e.copy(out=s_coff[:, :], in_=p4[0][0:8, 512:513]),
                        reads=[PB(0, 512), ph + "coffR"], writes=["coff"])
                    P.op("pe", lambda e: e.transpose(
                        p4[0][0:1, 520:528], s_coff[:, :], s_id[0:8, 0:8]),
                        reads=["coff", "id"], writes=[PB(0, 512), ph + "crowR"])
                    P.op("act", lambda e: e.copy(out=s_crow[:, :], in_=p4[0][0:1, 520:528]),
                        reads=[PB(0, 512), ph + "crowR"], writes=["crow"])
                    P.op("pe", lambda e: e.matmul(p4[0][:, 528:536], s_ones1[:, :], s_crow[:, :],
                                                  start=True, stop=True),
                        reads=["ones1", "crow"], writes=[PB(0, 512), ph + "cofbR"])
                    P.op("act", lambda e: e.copy(out=s_coffb[:, :], in_=p4[0][:, 528:536]),
                        reads=[PB(0, 512), ph + "cofbR"], writes=["coffb"])

                def b_hook(st):
                    if st == 16:
                        emit_coff_chain()
                    elif st == 18:
                        for t_ in range(NT):
                            y1(t_)
                    elif st == 28 and phase == 1 and PHASES["attn"]:
                        P.op("qg", lambda e: e.dma_start(
                            out=wbuf[1][:, :].rearrange("p (k f) -> p k f", k=8),
                            in_=dwqkv.rearrange("(k p) c -> p k c", p=128)[:, :, 2048:3072]),
                            writes=["wbig", ph + "wbuf1"])

                sw_pipe(NST, [b_load, b_tp, b_mm, b_ev, b_st, b_hook])

                yst_all = [ph + f"yst{st}" for st in range(NST)]

                ygb4 = [cgb[0], cgb[1], ybuf[0], ybuf[1]]

                def c_gather(t):
                    sl = t % 4
                    P.op("qg", lambda e, t=t, sl=sl: e.indirect_dma_start(
                        out=ygb4[sl][:, :], out_offset=None,
                        in_=y_scr.ap(),
                        in_offset=bass.IndirectOffsetOnAxis(ap=sidx[:, t:t + 1], axis=0),
                        bounds_check=REGS["sb"], oob_is_err=False),
                        reads=yst_all + [ph + f"sidx{t}"], writes=[ph + f"cg{sl}"])

                def c_add(t):
                    sl = t % 4
                    P.op("dve", lambda e, t=t, sl=sl: e.scalar_tensor_tensor(
                        out=x_sb[:, t * D:(t + 1) * D], in0=ygb4[sl][:, :],
                        scalar=c_all[:, t:t + 1], in1=x_sb[:, t * D:(t + 1) * D],
                        op0=ALU.mult, op1=ALU.add),
                        reads=[ph + f"cg{sl}", f"call{t}", f"xt{t}"], writes=[f"xt{t}"])

                def c_store(t):
                    P.op("qs", lambda e, t=t: e.dma_start(
                        out=dy[t * 128:(t + 1) * 128, :], in_=x_sb[:, t * D:(t + 1) * D]),
                        reads=[f"xt{t}"], writes=[f"out{t}"])

                nop = lambda t: None
                stages_c = [c_gather, nop, nop, c_add]
                if phase == 2:
                    stages_c.append(c_store)
                if phase == 1 and PHASES["attn"]:
                    l2l = [mk_l2_stages(t, t, xlnt[:, 0:8192], xlnt[:, 8192:16384],
                                        wbuf[1]) for t in range(8)]
                    nl = max(len(s) for s in l2l)
                    stages_c += [
                        lambda idx, j=j: l2l[idx][j]() if idx < 8 else None
                        for j in range(nl)]
                sw_pipe(NT, stages_c)
                if phase == 1:
                    return dict(xlnt=xlnt, wbuf=wbuf, foff=foff)

        def attn(soft=False, shared=None):
            As = contextlib.ExitStack()
            with As:
                def asb(name, shape, dt):
                    return As.enter_context(nc.sbuf_tensor(f"{name}_a", shape, dt))
                if shared:
                    foff = shared["foff"]
                    xln2 = shared["xlnt"][:, 0:8192]
                    v_bf = shared["xlnt"][:, 8192:16384]
                    wbig = shared["wbuf"][1]
                    w0 = shared["wbuf"][0]
                    qkb_ap = [w0[:, 0:2048], w0[:, 2048:4096]]
                    PTs = [w0[:, 7168 + i * 384: 7168 + (i + 1) * 384] for i in range(2)]
                    ma = foff["sidx"]
                    assert ma + 52736 <= 229344, f"manual attn region overflow {ma}"
                    s_cos = nc.alloc_sbuf_tensor_at("cos_a", [128, 1024], F32, offset=foff["gb0"])
                    s_sin = nc.alloc_sbuf_tensor_at("sin_a", [128, 1024], F32, offset=foff["zf0"])
                    q_bf = nc.alloc_sbuf_tensor_at("q_a", [128, 8192], BF16, offset=ma)
                    k_bf = nc.alloc_sbuf_tensor_at("k_a", [128, 8192], BF16, offset=ma + 16384)
                    aO = nc.alloc_sbuf_tensor_at("aO_a", [128, 8192], BF16, offset=ma + 32768)
                    def man4(nm, i0):
                        return [nc.alloc_sbuf_tensor_at(f"{nm}_{i}_a", [128, 2], F32,
                                offset=ma + 49152 + 32 * (i0 * 4 + i)) for i in range(4)]
                    rm2, nm2, rs2, rr2 = man4("rm2", 0), man4("nm2", 1), man4("rs2", 2), man4("rr2", 3)
                    Ps_bf = [nc.alloc_sbuf_tensor_at(f"Pbf{i}_a", [128, 384], BF16,
                             offset=ma + 49664 + 768 * i) for i in range(4)]
                else:
                    s_cos = asb("cos", [128, 1024], F32)
                    s_sin = asb("sin", [128, 1024], F32)
                    xln2 = asb("xln2", [128, KC * 1024], BF16)
                    wbig = asb("wbig", [128, 8192], BF16)
                    qkb = [asb(f"qkb{i}", [128, 2048], BF16) for i in range(2)]
                    qkb_ap = [qkb[0][:, :], qkb[1][:, :]]
                    q_bf = asb("q", [128, 8 * 1024], BF16)
                    aO = asb("aO", [128, 8 * 1024], BF16)
                    k_bf = asb("k", [128, 8 * 1024], BF16)
                    v_bf = asb("v", [128, 8 * 1024], BF16)
                    Ps_bf = [asb(f"Pbf{i}", [128, 384], BF16) for i in range(4)]
                    PTs = [asb(f"PT{i}", [128, 384], BF16) for i in range(2)]
                    rm2 = [asb(f"rm2_{i}", [128, 2], F32) for i in range(4)]
                    nm2 = [asb(f"nm2_{i}", [128, 2], F32) for i in range(4)]
                    rs2 = [asb(f"rs2_{i}", [128, 2], F32) for i in range(4)]
                    rr2 = [asb(f"rr2_{i}", [128, 2], F32) for i in range(4)]
                poslots = [(0, 512), (1, 0), (1, 512), (2, 0)]
                ph_gz = (["p1_gb0", "p1_gb1", "p1_zf0", "p1_zf1"] if soft else [])
                ph_wb = ["p1_wbuf0", "p1_wbuf1"] if soft else []

                P.op("dve", lambda e: e.memset(aO[:, :], 0.0),
                    reads=[f"xt{NT - 1}"] if soft else [],
                    writes=[f"aO{c}" for c in range(8)])
                for sg in range(NSG):
                    ph0 = ph_gz if sg == 0 else []
                    if not (shared and sg == 0):
                        P.op("qg", lambda e: e.dma_start(
                            out=wbig.ap().rearrange("p (k f) -> p k f", k=8),
                            in_=dwqkv.rearrange("(k p) c -> p k c", p=128)[:, :, 2048:3072]),
                            writes=["wbig"])
                    P.op("qs", lambda e, sg=sg: e.dma_start(
                        out=s_cos[:, :], in_=dcos[:, sg * 1024:(sg + 1) * 1024]),
                        writes=["cos"] + ph0)
                    P.op("qs", lambda e, sg=sg: e.dma_start(
                        out=s_sin[:, :], in_=dsin[:, sg * 1024:(sg + 1) * 1024]),
                        writes=["sin"])
                    if not (shared and sg == 0):
                        l2_lists = [mk_l2_stages(sg * 8 + ts, ts, xln2, v_bf, wbig)
                                    for ts in range(8)]
                        nst_l = max(len(s) for s in l2_lists)
                        sw_pipe(8, [lambda idx, j=j: l2_lists[idx][j]() if j < len(l2_lists[idx]) else None
                                    for j in range(nst_l)])
                    xl2all = [f"xl2_{ts}" for ts in range(8)]
                    for half, (base, dst, dkey) in enumerate(((0, q_bf, "q"), (8, k_bf, "k"))):
                        for pr in range(4):
                            ocA = base + 2 * pr
                            sl = pr % 2
                            ia, ib = (2 * pr) % 4, (2 * pr + 1) % 4
                            phq = ph_wb if (sg == 0 and half == 0 and pr < 2) else []
                            P.op("qg", lambda e, sl=sl, ocA=ocA: e.dma_start(
                                out=qkb_ap[sl].rearrange("p (k c) -> p k c", k=8),
                                in_=dwqkv.rearrange("(k p) c -> p k c", p=128)[:, :, ocA * 128:(ocA + 2) * 128]),
                                writes=[f"qkb{sl}"] + phq)
                            for j, pidx in ((0, ia), (1, ib)):
                                for k in range(KC):
                                    for os_ in range(2):
                                        P.op("pe", lambda e, j=j, sl=sl, k=k, os_=os_, pidx=pidx: e.matmul(
                                            p4[pidx][:, os_ * 512:(os_ + 1) * 512],
                                            qkb_ap[sl][:, k * 256 + j * 128: k * 256 + j * 128 + 128],
                                            xln2[:, k * 1024 + os_ * 512: k * 1024 + (os_ + 1) * 512],
                                            start=(k == 0), stop=(k == KC - 1)),
                                            reads=[f"qkb{sl}"] + xl2all,
                                            writes=[PB(pidx, os_ * 512)])
                            pA, pB = p4[ia], p4[ib]
                            pAk = [PB(ia, 0), PB(ia, 512)]
                            pBk = [PB(ib, 0), PB(ib, 512)]
                            cE = (ocA - base) * 1024
                            cO = cE + 1024
                            if not ATT_SUB["rope"]:
                                P.op("dve", lambda e, pA=pA, dst=dst, cE=cE: e.tensor_copy(
                                    out=dst[:, cE:cE + 1024], in_=pA[:, :]),
                                    reads=pAk, writes=[f"{dkey}{cE}"])
                                P.op("dve", lambda e, pB=pB, dst=dst, cO=cO: e.tensor_copy(
                                    out=dst[:, cO:cO + 1024], in_=pB[:, :]),
                                    reads=pBk, writes=[f"{dkey}{cO}"])
                                continue
                            P.op("dve", lambda e, pA=pA: e.tensor_tensor(
                                out=scrA[0][:, :], in0=pA[:, :], in1=s_cos[:, :], op=ALU.mult),
                                reads=pAk + ["cos"], writes=["scrA0"])
                            P.op("dve", lambda e, pB=pB: e.tensor_tensor(
                                out=scr_b[:, :], in0=pB[:, :], in1=s_sin[:, :], op=ALU.mult),
                                reads=pBk + ["sin"], writes=["scrb"])
                            P.op("dve", lambda e, dst=dst, cE=cE: e.tensor_tensor(
                                out=dst[:, cE:cE + 1024], in0=scrA[0][:, :], in1=scr_b[:, :],
                                op=ALU.subtract),
                                reads=["scrA0", "scrb"], writes=[f"{dkey}{cE}"])
                            P.op("dve", lambda e, pA=pA: e.tensor_tensor(
                                out=scrA[0][:, :], in0=pA[:, :], in1=s_sin[:, :], op=ALU.mult),
                                reads=pAk + ["sin"], writes=["scrA0"])
                            P.op("dve", lambda e, pB=pB: e.tensor_tensor(
                                out=scr_b[:, :], in0=pB[:, :], in1=s_cos[:, :], op=ALU.mult),
                                reads=pBk + ["cos"], writes=["scrb"])
                            P.op("dve", lambda e, dst=dst, cO=cO: e.tensor_tensor(
                                out=dst[:, cO:cO + 1024], in0=scr_b[:, :], in1=scrA[0][:, :],
                                op=ALU.add),
                                reads=["scrA0", "scrb"], writes=[f"{dkey}{cO}"])
                    ps_slots = [(3, 0), (3, 512), (2, 0), (2, 512)]
                    pt_slots = [(0, 0), (0, 512)]
                    po_slots = [(1, 0), (1, 512)]

                    def stage_qk(b, h):
                        s4 = h % 4
                        pi, pS_off = ps_slots[s4]
                        pS = p4[pi]
                        pSk = PB(pi, pS_off)
                        hb = 32 * (h % 4)
                        cE = 2 * (h // 4) * 1024
                        cO = cE + 1024
                        qr = [f"q{cE}", f"q{cO}"]
                        kr = [f"k{cE}", f"k{cO}"]
                        P.op("pe", lambda e, pS=pS, o=pS_off: e.matmul(
                            pS[:, o:o + 384], s_idbf[:, :], s_mask[:, :],
                            start=True, stop=False),
                            reads=["idbf", "mask"], writes=[pSk])
                        mm = [(0, 128, 0, 0), (0, 128, 1, 0), (1, 256, 0, 128), (1, 256, 1, 128)]
                        for n_, (qt, klen, xh, po) in enumerate(mm):
                            co = cE if xh == 0 else cO
                            P.op("pe", lambda e, pS=pS, o=pS_off, qt=qt, klen=klen, co=co, po=po, hb=hb, b=b, n_=n_: e.matmul(
                                pS[:, o + po:o + po + klen],
                                q_bf[hb:hb + 32, co + b * 256 + qt * 128: co + b * 256 + (qt + 1) * 128],
                                k_bf[hb:hb + 32, co + b * 256: co + b * 256 + klen],
                                start=False, stop=(n_ == 3), tile_position=(hb, 0)),
                                reads=qr + kr, writes=[pSk])

                    def stage_smA(b, h):
                        s4 = h % 4
                        pi, o = ps_slots[s4]
                        pS = p4[pi]
                        pSk = PB(pi, o)
                        P.op("dve", lambda e, pS=pS, o=o, s4=s4: e.reduce_max(
                            out=rm2[s4][:, 0:1], in_=pS[:, o:o + 128], axis=AX.X),
                            reads=[pSk], writes=[f"rm{s4}"])
                        P.op("dve", lambda e, pS=pS, o=o, s4=s4: e.reduce_max(
                            out=rm2[s4][:, 1:2], in_=pS[:, o + 128:o + 384], axis=AX.X),
                            reads=[pSk], writes=[f"rm{s4}"])
                        P.op("dve", lambda e, s4=s4: e.tensor_scalar(
                            out=nm2[s4][:, :], in0=rm2[s4][:, :], scalar1=-0.125,
                            scalar2=None, op0=ALU.mult),
                            reads=[f"rm{s4}"], writes=[f"nm{s4}"])

                    def stage_smB(b, h):
                        s4 = h % 4
                        pi, o = ps_slots[s4]
                        pS = p4[pi]
                        pSk = PB(pi, o)
                        P.op("act", lambda e, pS=pS, o=o, s4=s4: e.activation(
                            out=Ps_bf[s4][:, 0:128], in_=pS[:, o:o + 128], func=AF.Exp,
                            scale=0.125, bias=nm2[s4][:, 0:1], accum_out=rs2[s4][:, 0:1]),
                            reads=[pSk, f"nm{s4}"], writes=[f"Psb{s4}", f"rs{s4}"])
                        P.op("act", lambda e, pS=pS, o=o, s4=s4: e.activation(
                            out=Ps_bf[s4][:, 128:384], in_=pS[:, o + 128:o + 384], func=AF.Exp,
                            scale=0.125, bias=nm2[s4][:, 1:2], accum_out=rs2[s4][:, 1:2]),
                            reads=[pSk, f"nm{s4}"], writes=[f"Psb{s4}", f"rs{s4}"])

                    def stage_smC(b, h):
                        s4 = h % 4
                        P.op("dve", lambda e, s4=s4: e.reciprocal(out=rr2[s4][:, :], in_=rs2[s4][:, :]),
                            reads=[f"rs{s4}"], writes=[f"rr{s4}"])
                        P.op("dve", lambda e, s4=s4: e.tensor_scalar(
                            out=Ps_bf[s4][:, 0:128], in0=Ps_bf[s4][:, 0:128], scalar1=rr2[s4][:, 0:1],
                            scalar2=None, op0=ALU.mult),
                            reads=[f"Psb{s4}", f"rr{s4}"], writes=[f"Pbb{s4}"])
                        P.op("dve", lambda e, s4=s4: e.tensor_scalar(
                            out=Ps_bf[s4][:, 128:384], in0=Ps_bf[s4][:, 128:384], scalar1=rr2[s4][:, 1:2],
                            scalar2=None, op0=ALU.mult),
                            reads=[f"Psb{s4}", f"rr{s4}"], writes=[f"Pbb{s4}"])

                    def stage_smD(b, h):
                        s4 = h % 4
                        s2 = h % 2
                        pti, pToff = pt_slots[s2]
                        pTb = p4[pti][:, pToff:pToff + 256].bitcast(BF16)
                        pTk = PB(pti, pToff)
                        for i3 in range(3):
                            P.op("pe", lambda e, pTb=pTb, i3=i3, s4=s4: e.transpose(
                                pTb[:, i3 * 128:(i3 + 1) * 128],
                                Ps_bf[s4][:, i3 * 128:(i3 + 1) * 128], s_idbf[:, :]),
                                reads=[f"Pbb{s4}", "idbf"], writes=[pTk])
                        P.op("act", lambda e, pTb=pTb, s2=s2: e.copy(
                            out=PTs[s2][:, :], in_=pTb[:, 0:384]),
                            reads=[pTk], writes=[f"PTsb{s2}"])

                    def stage_smE(b, h):
                        s2 = h % 2
                        poi, poff = po_slots[s2]
                        pO = p4[poi]
                        pOk = PB(poi, poff)
                        hp = 64 * (h % 2)
                        P.op("pe", lambda e, pO=pO, poff=poff, hp=hp, b=b, h=h, s2=s2: e.matmul(
                            pO[hp:hp + 64, poff:poff + 256],
                            v_bf[:, (b * 2) * 1024 + 64 * h: (b * 2) * 1024 + 64 * h + 64],
                            PTs[s2][:, 0:256], start=True, stop=False,
                            tile_position=(0, hp)),
                            reads=[f"v{b*2}", f"PTsb{s2}"], writes=[pOk])
                        P.op("pe", lambda e, pO=pO, poff=poff, hp=hp, b=b, h=h, s2=s2: e.matmul(
                            pO[hp:hp + 64, poff + 128:poff + 256],
                            v_bf[:, (b * 2 + 1) * 1024 + 64 * h: (b * 2 + 1) * 1024 + 64 * h + 64],
                            PTs[s2][:, 256:384], start=False, stop=True,
                            tile_position=(0, hp)),
                            reads=[f"v{b*2+1}", f"PTsb{s2}"], writes=[pOk])
                        P.op("act", lambda e, pO=pO, poff=poff, hp=hp, h=h, b=b: e.copy(
                            out=aO[hp:hp + 64, (h // 2) * 1024 + b * 256: (h // 2) * 1024 + (b + 1) * 256],
                            in_=pO[hp:hp + 64, poff:poff + 256]),
                            reads=[pOk], writes=[f"aO{h//2}"])

                    heads = [(b, h) for b in range(4) for h in ATT_SUB["heads"]]
                    if ATT_SUB["core"]:
                        stages = [stage_qk, stage_smA, stage_smB, stage_smC, stage_smD, stage_smE]
                        n = len(heads)
                        for i in range(n + len(stages) - 1):
                            for j in range(len(stages) - 1, -1, -1):
                                idx = i - j
                                if 0 <= idx < n:
                                    stages[j](*heads[idx])
                    P.op("qg", lambda e: e.dma_start(
                        out=wbig.ap().rearrange("p (k f) -> p k f", k=8),
                        in_=dwout.rearrange("(k p) f -> p k f", p=128)),
                        writes=["wbig"])
                    for ts in range(8):
                        t = sg * 8 + ts
                        pw = p4[ts % 2]
                        pwk = [PB(ts % 2, 0), PB(ts % 2, 512)]
                        for k in range(KC):
                            for os_ in range(2):
                                P.op("pe", lambda e, pw=pw, k=k, ts=ts, os_=os_: e.matmul(
                                    pw[:, os_ * 512:(os_ + 1) * 512],
                                    aO[:, k * 1024 + ts * 128: k * 1024 + (ts + 1) * 128],
                                    wbig[:, k * 1024 + os_ * 512: k * 1024 + (os_ + 1) * 512],
                                    start=(k == 0), stop=(k == KC - 1)),
                                    reads=[f"aO{k}", "wbig"],
                                    writes=[pwk[os_]])
                        P.op("dve", lambda e, pw=pw, t=t: e.tensor_tensor(
                            out=x_sb[:, t * D:(t + 1) * D], in0=pw[:, :],
                            in1=x_sb[:, t * D:(t + 1) * D], op=ALU.add),
                            reads=pwk + [f"xt{t}"], writes=[f"xt{t}"])

        sh1 = None
        if PHASES["ffn1"]:
            sh1 = ffn(1)
            if not PHASES["attn"]:
                P.barrier()
        if PHASES["attn"]:
            attn(soft=PHASES["ffn1"], shared=sh1 if PHASES["ffn1"] else None)
            P.barrier()
        if PHASES["ffn2"]:
            ffn(2)
            P.barrier()

        if not PHASES["ffn2"]:
            store_idx = P.op("qs", lambda e: e.dma_start(
                out=dy.rearrange("(t p) f -> p t f", p=128),
                in_=x_sb.ap().rearrange("p (t f) -> p t f", t=NT)),
                reads=[f"xt{t}" for t in range(NT)], writes=["out"])
            P.ops[store_idx]["marked"] = True

        P.finalize()

        def qwait(eobj, waited, qeng, qidx):
            QD = QDEPTHS[qeng]
            s = qsems[qeng][qidx % QD]
            val = 16 * (qidx // QD + 1)
            key = (qeng, qidx % QD)
            if waited.get(key, 0) < val:
                eobj.wait_ge(s, val)
                waited[key] = val

        def P_emit(names, eobj):
            if isinstance(names, str):
                names = (names,)
            waited = {}
            issued = 0
            for o in P.ops:
                name = o["eng"]
                if name not in names:
                    continue
                if name in P.QE and o["sem"] is None:
                    if issued >= QDEPTHS[name]:
                        qwait(eobj, waited, name, issued - QDEPTHS[name])
                    issued += 1
                for d in sorted(o["fdeps"]):
                    dep = P.ops[d]
                    if dep["sem"] is not None:
                        key = id(dep["sem"])
                        if waited.get(key, 0) < 1:
                            eobj.wait_ge(dep["sem"], 1)
                            waited[key] = 1
                        continue
                    deng = dep["eng"]
                    if deng in P.QE:
                        qwait(eobj, waited, deng, dep["qidx"])
                        continue
                    val = dep["mcount"]
                    if val <= 0 or waited.get(deng, 0) >= val:
                        continue
                    eobj.wait_ge(sem[deng], val)
                    waited[deng] = val
                ins = o["fn"](eobj)
                if o["sem"] is not None:
                    ins.then_inc(o["sem"])
                elif o["marked"]:
                    if name in P.QE:
                        ins.then_inc(qsems[name][(issued - 1) % QDEPTHS[name]], 16)
                    else:
                        ins.then_inc(sem[name], 1)

        nst = sum(1 for o in P.ops if o["eng"] == "qs" and o["sem"] is None)

        @block.tensor
        def _(eng):
            P_emit("pe", eng)

        @block.scalar
        def _(eng):
            P_emit(("act", "qa"), eng)

        @block.vector
        def _(eng):
            P_emit("dve", eng)

        @block.sync
        def _(eng):
            P_emit("qs", eng)
            w = {}
            for i in range(min(QDEPTHS["qs"], nst)):
                qwait(eng, w, "qs", nst - 1 - i)

        @block.gpsimd
        def _(eng):
            r_sb = eng.alloc_register()
            REGS["sb"] = r_sb
            eng.reg_mov(REGS["sb"], NSLOT - 1)
            r_gb = eng.alloc_register()
            REGS["gb"] = r_gb
            eng.reg_mov(REGS["gb"], NSLOT)
            P_emit("qg", eng)

    return nc



_PERM = qk_col_perm()


def stage_inputs(L0_attn_x, L0_attn_Wqkv, L0_attn_Wout, L0_ffn_wg, L0_ffn_W, L0_ffn_b):
    x = np.ascontiguousarray(np.asarray(L0_attn_x, dtype=np.float32).reshape(B * L, D))
    wqkvT = np.ascontiguousarray(np.asarray(L0_attn_Wqkv, dtype=np.float32).T[:, _PERM])
    woutT = np.ascontiguousarray(np.asarray(L0_attn_Wout, dtype=np.float32).T)
    wexpT = np.ascontiguousarray(np.asarray(L0_ffn_W, dtype=np.float32).transpose(0, 2, 1))
    wg = np.ascontiguousarray(np.asarray(L0_ffn_wg, dtype=np.float32))
    bias = np.ascontiguousarray(np.asarray(L0_ffn_b, dtype=np.float32))
    ident, ident_bf, mask384, sel_bf = build_consts()
    eidx = np.tile(np.arange(8, dtype=np.float32)[None, :], (128, 1))
    tokbig = (65536.0 + np.arange(NT, dtype=np.float32)[None, :] * 128
              + np.arange(128, dtype=np.float32)[:, None]).astype(np.float32)
    ones1 = np.ones((1, 128), dtype=np.float32)
    zinit = np.zeros((NE * 512, D), dtype=BF16NP)
    maps = []
    for c in range(NCORES):
        cos, sin = rope_tables(c)
        rank = (np.arange(8) < c).astype(np.float32).reshape(8, 1)
        maps.append({
            "x": np.ascontiguousarray(x[c * TT:(c + 1) * TT]),
            "wqkvT": wqkvT, "woutT": woutT, "wexpT": wexpT, "wg": wg, "bias": bias,
            "cos": cos, "sin": sin, "ident": ident, "ident_bf": ident_bf,
            "mask384": mask384, "sel": sel_bf, "rankmask": rank,
            "eidx": eidx, "tokbig": tokbig, "ones1": ones1,
            "z_scr": zinit,
        })
    return maps


def kernel(L0_attn_x, L0_attn_Wqkv, L0_attn_Wout, L0_ffn_wg, L0_ffn_W, L0_ffn_b):
    maps = stage_inputs(L0_attn_x, L0_attn_Wqkv, L0_attn_Wout, L0_ffn_wg, L0_ffn_W, L0_ffn_b)
    nc = build_nc()
    res = run_bass_kernel_spmd(nc, maps, core_ids=list(range(NCORES)))
    out = np.concatenate([np.asarray(r["y"]) for r in res.results], axis=0)
    return out.reshape(B, L, D).astype(np.float32)
